# version 17
# baseline (speedup 1.0000x reference)
import numpy as np
import ml_dtypes
import concourse.bass as bass
import concourse.mybir as mybir
from concourse.bass_utils import run_bass_kernel_spmd
from concourse.alu_op_type import AluOpType as ALU

F32 = mybir.dt.float32
F16 = mybir.dt.float16
BF16 = mybir.dt.bfloat16
I32 = mybir.dt.int32
I16 = mybir.dt.int16
U16 = mybir.dt.uint16
FP8 = mybir.dt.float8e4
AF = mybir.ActivationFunctionType

S, E, H = 2048, 2048, 16
IFF = 5632
FGU = 2 * IFF
NQ = 1024
NTT, NST, ET = 8, 16, 16
GU_FT, GU_PAIRS, DN_HT, DN_IT = 88, 44, 16, 44
KB = 1024

ENGINES = ("T", "A", "V", "P", "S")
T, A, V, P, SY = ENGINES


class BK:
    def __init__(self, nc):
        self.nc = nc
        self.sem = {}
        self.cnt = {}
        self.ops = {e: [] for e in ENGINES}
        self.mark = {e: {} for e in ENGINES}

    def s(self, name):
        if name not in self.sem:
            self.sem[name] = self.nc.alloc_semaphore(name)
            self.cnt[name] = 0
        return self.sem[name]

    def emit(self, eng, fname, kwargs=None, wait=(), inc=None, incby=1):
        if inc:
            self.s(inc)
        ws = []
        for w in wait:
            if w is None:
                continue
            sn, v = w
            if v is None:
                continue
            self.s(sn)
            if self.mark[eng].get(sn, 0) >= v:
                continue
            self.mark[eng][sn] = v
            ws.append((sn, v))
        tok = None
        if inc is not None:
            self.cnt[inc] += incby
            tok = self.cnt[inc]
        self.ops[eng].append((fname, kwargs or {}, ws, inc, incby))
        return tok

    def dma(self, eng, semname, out, in_, wait=(), **kw):
        wait = list(wait)
        self.s(semname)
        if self.cnt[semname]:
            wait.append((semname, self.cnt[semname]))
        tok = self.emit(eng, "dma_start", dict(out=out, in_=in_, **kw),
                        wait=wait, inc=semname, incby=16)
        return (semname, tok)

    def gate(self):
        snap = [(sn, v) for sn, v in self.cnt.items()]
        for e in ENGINES:
            self.emit(e, "wait_only", {}, wait=snap)

    def replay(self, eng_obj, eng):
        for fname, kwargs, ws, inc, incby in self.ops[eng]:
            for sn, v in ws:
                eng_obj.wait_ge(self.sem[sn], v)
            if fname == "wait_only":
                continue
            r = getattr(eng_obj, fname)(**kwargs)
            if inc is not None:
                r.then_inc(self.sem[inc], incby)


def build_program():
    nc = bass.Bass("TRN2", target_bir_lowering=False)

    def din(name, shape, dt):
        return nc.dram_tensor(name, shape, dt, kind="ExternalInput").ap()

    xTd = din("xT", [E, S], F32)
    xTqd = din("xTq", [E, NQ], F32)
    xresd = din("xres", [NQ, E], F32)
    wqkvT = din("wqkvT", [E, 3 * E], F32)
    bqkvR = din("bqkvR", [1, 3 * E], F32)
    woutTd = din("woutT", [E, E], F32)
    boutR = din("boutR", [1, E], F32)
    cosRd = din("cosR", [128, E], F32)
    sinRd = din("sinSR", [128, E], F32)
    maskABd = din("maskAB", [128, 256], BF16)
    gup = din("gup", [FGU, E // 8], I32)
    gusX = din("gus", [FGU, 2048], F16)
    dnp = din("dnp", [E, IFF // 8], I32)
    dnsX = din("dns", [E, IFF], F16)
    outd = nc.dram_tensor("out", [NQ, E], F32, kind="ExternalOutput").ap()

    kTsd = nc.dram_tensor("kTs", [H, 128, S], BF16, kind="Internal").ap()
    vsd = nc.dram_tensor("vsd", [H, S, 128], BF16, kind="Internal").ap()
    x1sd = nc.dram_tensor("x1s", [NQ, E], F32, kind="Internal").ap()
    x1Tsd = nc.dram_tensor("x1Ts", [ET, 128, NQ], BF16, kind="Internal").ap()
    hTsd = nc.dram_tensor("hTs", [DN_IT, 128, NQ], BF16, kind="Internal").ap()

    MANUAL_SHIFT_KB = 12
    def at(name, off_kb, shape, dt):
        return nc.alloc_sbuf_tensor_at(name, shape, dt,
                                       offset=int((off_kb + MANUAL_SHIFT_KB) * KB)).ap()

    ones = nc.alloc_sbuf_tensor("ones", [128, 128], BF16).ap()
    cst = nc.alloc_sbuf_tensor("cst", [128, 8], F32).ap()
    maskA = nc.alloc_sbuf_tensor("maskA", [128, 128], BF16).ap()
    maskB = nc.alloc_sbuf_tensor("maskB", [128, 128], BF16).ap()
    st6 = nc.alloc_sbuf_tensor("st6", [128, 24], F32).ap()
    ag2 = nc.alloc_sbuf_tensor("ag2", [128, 2], F32).ap()
    rr = nc.alloc_sbuf_tensor("rr", [128, 4], F32).ap()
    st6y = nc.alloc_sbuf_tensor("st6y", [128, 8 * 24], F32).ap()

    cosSB = at("cosSB", 16, [128, E], F32)
    sinSB = at("sinSB", 24, [128, E], F32)
    xrs = [at("xrs0", 16, [128, E], F32), at("xrs1", 24, [128, E], F32),
           at("xrs2", 32, [128, E], F32), at("xrs3", 40, [128, E], F32)]
    dnf8 = at("dnf8", 16, [128, 352], I32)
    dnsb = at("dnsb", 17.5, [128, 352], I32)
    dnpk = at("dnpk", 19, [128, 176], I32)
    dnA = at("dnA", 20, [128, 1408], BF16)
    dnWQ = [at("dnWQ0", 29, [128, 1408], BF16), at("dnWQ1", 26, [128, 1408], BF16)]
    x1rl = [at("x1rl0", 16, [128, 2048], F32), at("x1rl1", 24, [128, 2048], F32),
            at("x1rl2", 32, [128, 2048], F32), at("x1rl3", 40, [128, 2048], F32)]

    stg = [at("stg0", 56, [128, ET * 128], F32), at("stg1", 64, [128, ET * 128], F32),
           at("stg2", 40, [128, ET * 128], F32), at("stg3", 48, [128, ET * 128], F32)]
    ewk0 = at("ewk0", 32, [128, E], F32)
    ewk1 = at("ewk1", 40, [128, E], F32)
    kstg = at("kstg", 48, [128, E], BF16)
    ktr = at("ktr", 52, [128, E], BF16)
    ewk0b = at("ewk0b", 56, [128, E], F32)
    xq = [at(f"xq{i}", 64 + 4 * i, [128, ET * 128], BF16) for i in range(2)]
    vstg = [at(f"vstg{i}", 52 + i, [128, 512], BF16) for i in range(4)]
    kh = [at("kh0", 32, [128, S], BF16), at("kh1", 36, [128, S], BF16)]
    vh = [at("vh0", 40, [128, H * 128], BF16), at("vh1", 44, [128, H * 128], BF16)]
    mex = [at(f"mex{i}", (48, 50, 56, 58)[i], [128, NQ], BF16) for i in range(4)]
    zrS = at("zrS", 52, [128, NQ], F32)
    x1bf = at("x1bf", 48, [128, E], BF16)
    x1Tst = at("x1Tst", 52, [128, E], BF16)
    silwk = [at("silwk0", 32, [128, NQ], F32), at("silwk1", 36, [128, NQ], F32)]
    hstg = [at("hstg0", 40, [128, NQ], BF16), at("hstg1", 42, [128, NQ], BF16)]
    chk = [at("chk0", 112, [128, 11 * NQ], BF16), at("chk1", 32, [128, 11 * NQ], BF16),
           at("chk2", 54, [128, 11 * NQ], BF16)]
    yres = at("yres", 140, [128, 8 * 2048], F32)
    outst = [at("outst0", 54, [128, 2048], F32), at("outst1", 62, [128, 2048], F32)]

    gsc = [at("gsc0", 204, [128, 2048], F16), at("gsc1", 98, [128, 2048], F16)]
    dsc = [at("dsc0", 102, [128, 1408], F16), at("dsc1", 105, [128, 1408], F16)]
    guf8 = at("guf8", 57, [128, 512], I32)
    gusb = at("gusb", 59, [128, 512], I32)
    guA = at("guA", 61, [128, 2048], BF16)
    guWQ = [at("guWQ0", 65, [128, 2048], BF16), at("guWQ1", 44, [128, 2048], BF16)]
    gupk = at("gupk", 56, [128, 256], I32)
    bq = [at(f"bq{j}", 72 + j, [1, 512], BF16) for j in range(4)]

    wres = at("wres", 76, [128, 4 * ET * 512], BF16)
    avn = at("avn", 76, [128, H * NQ], BF16)
    wocol = [at("wocol0", 140, [128, ET * 512], BF16),
             at("wocol1", 156, [128, ET * 512], BF16),
             at("wocol2", 108, [128, ET * 512], BF16),
             at("wocol3", 124, [128, ET * 512], BF16)]
    qT = at("qT", 140, [128, H * NQ], BF16)
    xTb = at("xTb", 140, [128, ET * S], BF16)
    x1T = at("x1T", 140, [128, NTT * E], BF16)

    NRING = 8
    ring = [at(f"ring{i}", 172 + 4 * i, [128, E], BF16) for i in range(NRING)]
    wdq = [at("wdq0", 76, [128, 4 * 1408], BF16), at("wdq1", 87, [128, 4 * 1408], BF16)]

    ps = [nc.alloc_psum_tensor(f"ps{i}", [128, 1024], F32).ap() for i in range(4)]
    pslot = [ps[i // 2][:, (i % 2) * 512:(i % 2) * 512 + 512] for i in range(8)]

    bk = BK(nc)

    bk.dma(SY, "dCst", maskA, maskABd[:, 0:128])
    t_msk = bk.dma(SY, "dCst", maskB, maskABd[:, 128:256])
    t_cos = bk.dma(SY, "dCst", cosSB, cosRd[:])
    t_sin = bk.dma(SY, "dCst", sinSB, sinRd[:])
    for col, val in [(0, 1e-24), (1, -0.5), (2, float(np.log(2048.0 ** -0.25))),
                     (3, 1e-5), (4, 1e-6), (5, -1.0), (6, 64.0)]:
        bk.emit(V, "memset", dict(ap=cst[:, col:col + 1], constant=val), inc="vE")
    t_cst = bk.emit(V, "memset", dict(ap=ones, constant=1.0), inc="vE")

    dq = dict(gu_next=0, dn_next=0,
              gu_ready=[None] * GU_FT, gu_slot=[None] * GU_FT,
              ring_free=[None] * NRING,
              gu_pk_free=[], gu_wq_free=[None, None],
              dn_ready={}, dn_grp_free=[None, None],
              dn_pk_free=[], dn_wq_free=[None, None],
              gu_a_free=None, dn_a_free=None,
              gu_sc_free=[None, None], dn_sc_free=[None, None],
              ph6=False)

    def gu_order(i):
        return (i // 2) if i % 2 == 0 else GU_PAIRS + (i // 2)

    def _dq_pipe(pk, nw, f8, sbb, Ab, WQ, s_ap, t_sc, t_load, wq_free, a_free):
        nb = 4 * nw
        t1 = bk.emit(V, "tensor_scalar",
                     dict(out=f8[:, 0:nw], in0=pk, scalar1=2, scalar2=0x1C1C1C1C,
                          op0=ALU.logical_shift_left, op1=ALU.bitwise_and),
                     wait=[t_load], inc="vE")
        t1b = bk.emit(V, "tensor_scalar",
                      dict(out=f8[:, nw:2 * nw], in0=pk, scalar1=2, scalar2=0x1C1C1C1C,
                           op0=ALU.logical_shift_right, op1=ALU.bitwise_and),
                      wait=[("vE", t1)], inc="vE")
        t2 = bk.emit(V, "tensor_scalar",
                     dict(out=sbb[:, 0:nw], in0=pk, scalar1=0x08080808, scalar2=4,
                          op0=ALU.bitwise_and, op1=ALU.logical_shift_left),
                     wait=[t_load], inc="vE")
        t3 = bk.emit(V, "tensor_scalar",
                     dict(out=sbb[:, nw:2 * nw], in0=pk, scalar1=4, scalar2=0x08080808,
                          op0=ALU.logical_shift_right, op1=ALU.bitwise_and),
                     wait=[("vE", t2)], inc="vE")
        t4 = bk.emit(V, "tensor_scalar",
                     dict(out=sbb[:, nw:2 * nw], in0=sbb[:, nw:2 * nw], scalar1=4,
                          scalar2=None, op0=ALU.logical_shift_left),
                     wait=[("vE", t3)], inc="vE")
        t6 = bk.emit(V, "tensor_tensor",
                     dict(out=f8, in0=f8, in1=sbb, op=ALU.bitwise_or),
                     wait=[("vE", t1b), ("vE", t4)], inc="vE")
        t7 = bk.emit(A, "activation",
                     dict(out=Ab, in_=f8.bitcast(FP8), func=AF.Identity,
                          scale=cst[:, 6:7]),
                     wait=[("vE", t6), a_free, ("vE", t_cst)], inc="aE")
        t12 = bk.emit(V, "tensor_tensor",
                      dict(out=WQ, in0=Ab, in1=s_ap, op=ALU.mult),
                      wait=[("aE", t7), wq_free, t_sc], inc="vE")
        wq_toks = [("vE", t12)]
        return [("vE", t3)], wq_toks, ("vE", t12)

    def emit_gu_unit():
        i = dq["gu_next"]
        if i >= GU_FT:
            return False
        slot = i % NRING
        if i >= NRING and dq["ring_free"][slot] is None:
            return False
        dq["gu_next"] = i + 1
        ft = gu_order(i)
        wb = (i % 2) if dq["ph6"] else 0
        tl = bk.dma(P, "dSg", gupk, gup[128 * ft:128 * (ft + 1), :],
                    wait=dq["gu_pk_free"])
        tsc_ld = bk.dma(A, f"dScg{wb}", gsc[wb], gusX[128 * ft:128 * (ft + 1), :],
                        wait=[dq["gu_sc_free"][wb]])
        pk_free, wq, af = _dq_pipe(gupk, 256, guf8, gusb, guA, guWQ[wb],
                                   gsc[wb], tsc_ld, tl,
                                   dq["gu_wq_free"][wb], dq["gu_a_free"])
        dq["gu_a_free"] = af
        dq["gu_sc_free"][wb] = wq[0]
        tt = bk.dma(SY, f"dTr{slot}", ring[slot].rearrange("p (t c) -> p t c", t=ET),
                    guWQ[wb], transpose=True,
                    wait=wq + [("tE", dq["ring_free"][slot])])
        dq["gu_pk_free"] = pk_free
        dq["gu_wq_free"][wb] = tt
        dq["gu_ready"][ft] = tt
        dq["gu_slot"][ft] = slot
        return True

    def emit_dn_unit():
        i = dq["dn_next"]
        if i >= DN_HT * 4:
            return False
        if i >= 8 and dq["dn_grp_free"][(i // 4) % 2] is None:
            return False
        dq["dn_next"] = i + 1
        c, qq, j = i // 16, (i // 4) % 4, i % 4
        ht = 4 * c + j
        grp = (i // 4) % 2
        wb = i % 2
        tl = bk.dma(P, "dSd", dnpk, dnp[128 * ht:128 * (ht + 1), 176 * qq:176 * (qq + 1)],
                    wait=dq["dn_pk_free"])
        tsc_ld = bk.dma(A, f"dScd{wb}", dsc[wb],
                        dnsX[128 * ht:128 * (ht + 1), 1408 * qq:1408 * (qq + 1)],
                        wait=[dq["dn_sc_free"][wb]])
        pk_free, wq, af = _dq_pipe(dnpk, 176, dnf8, dnsb, dnA, dnWQ[wb],
                                   dsc[wb], tsc_ld, tl,
                                   dq["dn_wq_free"][wb], dq["dn_a_free"])
        dq["dn_a_free"] = af
        dq["dn_sc_free"][wb] = wq[0]
        dst = wdq[grp].rearrange("p (t j c) -> p t j c", t=11, j=4)[:, :, j]
        tt = bk.dma(SY, f"dTw{grp}", dst, dnWQ[wb], transpose=True,
                    wait=wq + [("tE", dq["dn_grp_free"][grp])])
        dq["dn_pk_free"] = pk_free
        dq["dn_wq_free"][wb] = tt
        dq["dn_ready"][(c, qq, j)] = tt
        return True

    xT_r = xTb.rearrange("p (et s) -> p et s", et=ET)
    xTd_r = xTd.rearrange("(et p) s -> p et s", p=128)
    wres_r = wres.rearrange("p (j et f) -> p j et f", j=4, et=ET)

    def load_col(dst4, srcd, g, wait=()):
        toks = []
        for c in range(4):
            srcc = (srcd[:, 512 * g + 128 * c:512 * g + 128 * (c + 1)]
                    .rearrange("(et p) f -> p et f", p=128))
            toks.append(bk.dma(P, f"dPk{c}", dst4[:, :, 128 * c:128 * (c + 1)],
                               srcc, wait=wait))
        return toks

    kcol_tok = [None] * 4
    kbq_tok = [None] * 4
    stg_cast = [None] * 4
    for j in range(4):
        toks = []
        for q in range(4):
            srcq = (wqkvT[:, 512 * (4 + j) + 128 * q:512 * (4 + j) + 128 * (q + 1)]
                    .rearrange("(et p) f -> p et f", p=128))
            tl = bk.dma(SY, f"dSw{q}", stg[q].rearrange("p (et f) -> p et f", et=ET),
                        srcq, wait=[stg_cast[q]])
            tc = bk.emit(A, "copy",
                         dict(out=wres_r[:, j, :, 128 * q:128 * (q + 1)],
                              in_=stg[q].rearrange("p (et f) -> p et f", et=ET)),
                         wait=[tl], inc="aE")
            stg_cast[q] = ("aE", tc)
            toks.append(("aE", tc))
        kcol_tok[j] = toks
        kbq_tok[j] = bk.dma(P, "dBq", bq[j], bqkvR[0:1, 512 * (4 + j):512 * (5 + j)])

    t_xTt = []
    for st in range(NST):
        ta = bk.dma(P, f"dP{(2 * st) % 4}", xT_r[:, 0:8, 128 * st:128 * (st + 1)],
                    xTd_r[:, 0:8, 128 * st:128 * (st + 1)])
        tb = bk.dma(P, f"dP{(2 * st + 1) % 4}", xT_r[:, 8:16, 128 * st:128 * (st + 1)],
                    xTd_r[:, 8:16, 128 * st:128 * (st + 1)])
        t_xTt.append((ta, tb))

    pfree = [None] * 8
    ewkbufs = [ewk0, ewk0b]
    rn = dict(ewk_free=[None, stg_cast[0]], g_free=stg_cast[2], rr_free=None,
              stage_free=stg_cast[3], st6_free=None)

    def proj_chains(st, lhs_fn, rhs_fn, w0_fn):
        evac_last = None
        mm_fc = [None] * 4
        ewkP = ewkbufs[st % 2]
        for fc in range(4):
            sl = (st % 2) * 4 + fc
            w0 = list(w0_fn(fc)) + [pfree[sl]]
            bk.emit(T, "matmul",
                    dict(out=pslot[sl], lhsT=ones[0:1, :],
                         rhs=bq[fc][0:1, :],
                         start=True, stop=False), wait=w0)
            mm = None
            for et in range(ET):
                mm = bk.emit(T, "matmul",
                             dict(out=pslot[sl], lhsT=lhs_fn(et),
                                  rhs=rhs_fn(fc, et),
                                  start=False, stop=(et == ET - 1)),
                             inc=("tE" if et == ET - 1 else None))
            mm_fc[fc] = mm
            evac_last = bk.emit(A, "copy",
                                dict(out=ewkP[:, 512 * fc:512 * (fc + 1)], in_=pslot[sl]),
                                wait=[("tE", mm)] +
                                     ([rn["ewk_free"][st % 2]] if fc == 0 else []),
                                inc="aE")
            pfree[sl] = ("aE", evac_last)
        return evac_last, mm_fc

    def rope_l2(st, evac_last, stage):
        ewkP = ewkbufs[st % 2]
        tg1 = bk.emit(V, "tensor_copy", dict(out=ewk1[:, 0:1024], in_=ewkP[:, 1:2048:2]),
                      wait=[("aE", evac_last), rn["g_free"]], inc="vE")
        tg2 = bk.emit(V, "tensor_copy", dict(out=ewk1[:, 1024:2048], in_=ewkP[:, 0:2048:2]),
                      wait=[("vE", tg1)], inc="vE")
        tg3 = bk.emit(V, "tensor_tensor", dict(out=ewk1, in0=ewk1, in1=sinSB, op=ALU.mult),
                      wait=[("vE", tg2), t_sin], inc="vE")
        tx1 = bk.emit(V, "tensor_tensor", dict(out=ewkP, in0=ewkP, in1=cosSB, op=ALU.mult),
                      wait=[("vE", tg2), t_cos], inc="vE")
        tro = bk.emit(V, "tensor_tensor", dict(out=ewkP, in0=ewkP, in1=ewk1, op=ALU.add),
                      wait=[("vE", tg3), ("vE", tx1)], inc="vE")
        rn["g_free"] = ("vE", tro)
        tb = None
        for ck in range(4):
            tb = bk.emit(V, "bn_stats",
                         dict(out=st6[:, 6 * ck:6 * ck + 6],
                              in_=ewkP[:, 512 * ck:512 * (ck + 1)]),
                         wait=([("vE", tro), rn["st6_free"]] if ck == 0 else [("vE", tb)]),
                         inc="vE")
        ta = bk.emit(V, "bn_aggr", dict(out=ag2, in_=st6), wait=[("vE", tb)], inc="vE")
        tu = bk.emit(V, "tensor_scalar",
                     dict(out=rr[:, 0:1], in0=ag2[:, 0:1], scalar1=ag2[:, 0:1],
                          scalar2=ag2[:, 1:2], op0=ALU.mult, op1=ALU.add),
                     wait=[("vE", ta), rn["rr_free"]], inc="vE")
        tu2 = bk.emit(V, "tensor_scalar",
                      dict(out=rr[:, 0:1], in0=rr[:, 0:1], scalar1=2048.0,
                           scalar2=cst[:, 0:1], op0=ALU.mult, op1=ALU.max),
                      wait=[("vE", tu)], inc="vE")
        tl1 = bk.emit(A, "activation", dict(out=rr[:, 1:2], in_=rr[:, 0:1], func=AF.Ln),
                      wait=[("vE", tu2)], inc="aE")
        tl2 = bk.emit(A, "activation",
                      dict(out=rr[:, 2:3], in_=rr[:, 1:2], func=AF.Exp,
                           scale=cst[:, 1:2], bias=cst[:, 2:3]),
                      wait=[("aE", tl1)], inc="aE")
        rn["rr_free"] = ("aE", tl2)
        tsc = bk.emit(V, "tensor_scalar",
                      dict(out=stage, in0=ewkP, scalar1=rr[:, 2:3], scalar2=None,
                           op0=ALU.mult),
                      wait=[("aE", tl2), rn["stage_free"]], inc="vE")
        rn["ewk_free"][st % 2] = ("vE", tsc)
        rn["st6_free"] = ("vE", tu2)
        return tsc

    k_last = [None] * 4
    kpendq = []
    ktr_free = [None]

    def finish_k_tile():
        if not kpendq:
            return
        st, ev = kpendq.pop(0)
        tsc = rope_l2(st, ev, kstg)
        ttr = bk.dma(SY, "dTk", ktr.rearrange("p (t c) -> p t c", t=ET), kstg,
                     transpose=True, wait=[("vE", tsc), ktr_free[0]])
        rn["stage_free"] = ttr
        tsp = bk.dma(A, "dAk",
                     kTsd[:, :, 128 * st:128 * (st + 1)].rearrange("h p c -> p h c"),
                     ktr.rearrange("p (h c) -> p h c", h=H),
                     wait=[ttr])
        ktr_free[0] = tsp

    for st in range(NST):
        ev, mfc = proj_chains(
            st,
            lambda et, st=st: xT_r[:, et, 128 * st:128 * (st + 1)],
            lambda fc, et: wres_r[:, fc, et],
            lambda fc, st=st: kcol_tok[fc] + [kbq_tok[fc], *t_xTt[st], ("vE", t_cst)])
        k_last = mfc
        finish_k_tile()
        kpendq.append((st, ev))
    finish_k_tile()

    vcol_tok = [None] * 4
    vbq_tok = [None] * 4
    vstg_free = [None] * 4
    v_last = [None] * 4
    ch = 0
    for fc in range(4):
        vcol_tok[fc] = load_col(wres_r[:, fc], wqkvT, 8 + fc,
                                wait=[("tE", k_last[fc])])
        vbq_tok[fc] = bk.dma(P, "dBq", bq[fc],
                             bqkvR[0:1, 512 * (8 + fc):512 * (9 + fc)],
                             wait=[("tE", k_last[fc])])
        mm = None
        for st in range(NST):
            sl = ch % 8
            ch += 1
            w0 = vcol_tok[fc] + [vbq_tok[fc], *t_xTt[st], pfree[sl], ("vE", t_cst)]
            bk.emit(T, "matmul",
                    dict(out=pslot[sl], lhsT=ones[0:1, :],
                         rhs=bq[fc][0:1, :],
                         start=True, stop=False), wait=w0)
            mm = None
            for et in range(ET):
                mm = bk.emit(T, "matmul",
                             dict(out=pslot[sl], lhsT=xT_r[:, et, 128 * st:128 * (st + 1)],
                                  rhs=wres_r[:, fc, et],
                                  start=False, stop=(et == ET - 1)),
                             inc=("tE" if et == ET - 1 else None))
            par = st % 4
            wv = [("tE", mm), vstg_free[par]]
            if fc == 0 and st < 4:
                wv.append(ktr_free[0])
            te = bk.emit(A, "copy", dict(out=vstg[par], in_=pslot[sl]),
                         wait=wv, inc="aE")
            pfree[sl] = ("aE", te)
            tsp = bk.dma(A, f"dAv{par}",
                         vsd[4 * fc:4 * (fc + 1), 128 * st:128 * (st + 1), :]
                         .rearrange("h s d -> s h d"),
                         vstg[par].rearrange("p (h d) -> p h d", h=4),
                         wait=[("aE", te)])
            vstg_free[par] = tsp
        v_last[fc] = mm

    qcol_tok = [None] * 4
    qbq_tok = [None] * 4
    for j in range(4):
        qcol_tok[j] = load_col(wres_r[:, j], wqkvT, j,
                               wait=[("tE", v_last[j])])
        qbq_tok[j] = bk.dma(P, "dBq", bq[j], bqkvR[0:1, 512 * j:512 * (j + 1)],
                            wait=[("tE", v_last[j])])

    xq_tok = [None] * NTT
    xq_free = [None, None]
    xTqd_r = xTqd.rearrange("(et p) t -> p et t", p=128)

    def load_xq(tt):
        b = tt % 2
        xqr = xq[b].rearrange("p (et c) -> p et c", et=ET)
        w = [xq_free[b], stg_cast[1]]
        ta = bk.dma(P, "dPv0", xqr[:, 0:8], xTqd_r[:, 0:8, 128 * tt:128 * (tt + 1)],
                    wait=w)
        tb = bk.dma(P, "dPv1", xqr[:, 8:16], xTqd_r[:, 8:16, 128 * tt:128 * (tt + 1)],
                    wait=w)
        xq_tok[tt] = (ta, tb)

    for tt in range(2):
        load_xq(tt)
    qT_h = qT.rearrange("p (h q) -> p h q", h=H)
    t_qlast = [None]
    q_last = [None] * 4
    qpend = [None]

    def finish_q_tile():
        if qpend[0] is None:
            return
        tt, ev = qpend[0]
        qpend[0] = None
        tsc = rope_l2(tt, ev, kstg)
        ttr = bk.dma(SY, "dTq", qT_h[:, :, 128 * tt:128 * (tt + 1)], kstg,
                     transpose=True,
                     wait=[("vE", tsc)] + ([("tE", v_last[3])] if tt == 0 else []))
        rn["stage_free"] = ttr
        t_qlast[0] = ttr

    for tt in range(NTT):
        ev, mfc = proj_chains(
            tt,
            lambda et, b=tt % 2: xq[b][:, 128 * et:128 * (et + 1)],
            lambda fc, et: wres_r[:, fc, et],
            lambda fc, tt=tt: qcol_tok[fc] + [qbq_tok[fc], *xq_tok[tt]])
        q_last = mfc
        xq_free[tt % 2] = ("tE", mfc[3])
        if tt + 2 < NTT:
            load_xq(tt + 2)
        finish_q_tile()
        qpend[0] = (tt, ev)
    finish_q_tile()
    t_qlast = t_qlast[0]

    bk.gate()

    wocol_r = [wocol[fc].rearrange("p (et f) -> p et f", et=ET) for fc in range(4)]
    wo_tok = [None] * 4
    wbq_tok = [None] * 4
    for fc in (2, 3):
        wo_tok[fc] = load_col(wocol_r[fc], woutTd, fc, wait=[("tE", q_last[fc])])
    for fc in range(4):
        wbq_tok[fc] = bk.dma(P, "dBq", bq[fc], boutR[0:1, 512 * fc:512 * (fc + 1)],
                             wait=[("tE", q_last[fc])])

    kh_tok = [None, None]
    vh_tok = [None, None]
    kh_free = [None, None]
    vh_free = [None, None]

    def prefetch_head(h):
        par = h % 2
        kh_tok[par] = bk.dma(A, f"dAh{par}", kh[par], kTsd[h], wait=[kh_free[par]])
        vh_tok[par] = bk.dma(A, f"dAvh{par}", vh[par].rearrange("p (t d) -> p t d", t=NST),
                             vsd[h].rearrange("(t p) d -> p t d", p=128),
                             wait=[vh_free[par]])

    prefetch_head(0)
    prefetch_head(1)

    av_free = None
    zr_free = None
    sc_free = [None, None]
    mex_free = [None] * 4

    def segs_for(kt):
        voff = 128 * (kt // 2)
        segs = []
        if kt % 2 == 1:
            segs.append((voff, 128, True))
            lo = voff + 128
        else:
            lo = voff
        if kt == NST - 1:
            return voff, segs
        if lo < 512:
            segs.append((lo, 512 - lo, False))
            segs.append((512, 512, False))
        elif lo < NQ:
            segs.append((lo, NQ - lo, False))
        return voff, segs

    for h in range(H):
        par = h % 2
        sc_tok = {}
        exp_tok = {}
        msk_tok = {}
        av_last = None
        z_last = None
        for step in range(NST + 3):
            if step < NST:
                kt = step
                p_sc = ps[kt % 2]
                voff = 128 * (kt // 2)
                segs = ([(voff, 512 - voff), (512, 512)] if voff < 512
                        else [(voff, NQ - voff)])
                mm = None
                for (o, n) in segs:
                    rhs = qT[:, NQ * h + o:NQ * h + o + n]
                    mm = bk.emit(T, "matmul",
                                 dict(out=p_sc[:, o:o + n],
                                      lhsT=kh[par][:, 128 * kt:128 * (kt + 1)],
                                      rhs=rhs, start=True, stop=True),
                                 wait=[kh_tok[par], t_qlast,
                                       sc_free[kt % 2]] if (o, n) == segs[0] else (),
                                 inc=("tE" if (o, n) == segs[-1] else None))
                sc_tok[kt] = mm
                et_ = bk.emit(A, "activation",
                              dict(out=mex[kt % 4][:, voff:NQ], in_=p_sc[:, voff:NQ],
                                   func=AF.Exp),
                              wait=[("tE", mm), mex_free[kt % 4]], inc="aE")
                exp_tok[kt] = et_
                sc_free[kt % 2] = ("aE", et_)
                mk = maskA if kt % 2 == 0 else maskB
                mt = bk.emit(V, "tensor_tensor",
                             dict(out=mex[kt % 4][:, voff:voff + 128],
                                  in0=mex[kt % 4][:, voff:voff + 128],
                                  in1=mk, op=ALU.mult),
                             wait=[("aE", et_), t_msk], inc="vE")
                msk_tok[kt] = mt
            if step >= 3:
                kt = step - 3
                voff, segs = segs_for(kt)
                first = (kt == 0)
                lastk = (kt == NST - 1)
                for (o, n, stp) in segs:
                    w = [("vE", msk_tok[kt]), vh_tok[par]]
                    if first and (o, n, stp) == segs[0]:
                        w += [av_free]
                    av_last = bk.emit(T, "matmul",
                                      dict(out=ps[2][:, o:o + n],
                                           lhsT=vh[par][:, 128 * kt:128 * (kt + 1)],
                                           rhs=mex[kt % 4][:, o:o + n],
                                           start=first, stop=stp,
                                           skip_group_check=True),
                                      wait=w,
                                      inc=("tE" if lastk and (o, n, stp) == segs[-1] else None))
                for (o, n, stp) in segs:
                    w = [("vE", msk_tok[kt])]
                    if first and (o, n, stp) == segs[0]:
                        w += [zr_free, ("vE", t_cst)]
                    z_last = bk.emit(T, "matmul",
                                     dict(out=ps[3][:, o:o + n],
                                          lhsT=ones,
                                          rhs=mex[kt % 4][:, o:o + n],
                                          start=first, stop=stp,
                                          skip_group_check=True),
                                     wait=w,
                                     inc=("tE" if lastk and (o, n, stp) == segs[-1] else None))
                mex_free[kt % 4] = ("tE", z_last) if z_last else None
        tln = bk.emit(A, "activation", dict(out=zrS, in_=ps[3], func=AF.Ln),
                      wait=[("tE", z_last)], inc="aE")
        tze = bk.emit(A, "activation", dict(out=zrS, in_=zrS, func=AF.Exp,
                                            scale=cst[:, 5:6]),
                      wait=[("aE", tln)], inc="aE")
        zr_free = ("aE", tln)
        tav = bk.emit(V, "tensor_tensor",
                      dict(out=avn[:, NQ * h:NQ * (h + 1)], in0=ps[2], in1=zrS,
                           op=ALU.mult),
                      wait=[("tE", av_last), ("aE", tze)], inc="vE")
        av_free = ("vE", tav)
        zr_free = ("vE", tav)
        kh_free[par] = ("tE", sc_tok[NST - 1])
        vh_free[par] = ("tE", av_last)
        if h + 2 < H:
            prefetch_head(h + 2)
        if h == H - 1:
            last_scores = sc_tok[NST - 1]

    for fc in (0, 1):
        wo_tok[fc] = load_col(wocol_r[fc], woutTd, fc, wait=[("tE", last_scores)])

    bk.gate()

    xrs_free = [None] * 4
    x1bf_free = None
    x1Tst_free = None
    pfree5 = [None] * 8
    ln_free = dict(st6=None, rr=None)
    for ttq in range(2):
        tts = [4 * ttq + i for i in range(4)]
        xr_tok = [None] * 4
        for i, tt in enumerate(tts):
            xr_tok[i] = bk.dma(A, f"dAx{i % 2}", xrs[i],
                               xresd[128 * tt:128 * (tt + 1), :],
                               wait=[xrs_free[i]])
        for fc in (2, 3, 0, 1):
            for i, tt in enumerate(tts):
                sl = (fc % 2) * 4 + i
                w0 = wo_tok[fc] + [wbq_tok[fc], pfree5[sl]]
                bk.emit(T, "matmul",
                        dict(out=pslot[sl], lhsT=ones[0:1, :],
                             rhs=bq[fc][0:1, :],
                             start=True, stop=False), wait=w0)
                mm = None
                for et in range(ET):
                    mm = bk.emit(T, "matmul",
                                 dict(out=pslot[sl],
                                      lhsT=avn[:, NQ * et + 128 * tt:NQ * et + 128 * tt + 128],
                                      rhs=wocol[fc][:, 512 * et:512 * (et + 1)],
                                      start=False, stop=(et == ET - 1)),
                                 inc=("tE" if et == ET - 1 else None))
                tv = bk.emit(V, "tensor_tensor",
                             dict(out=xrs[i][:, 512 * fc:512 * (fc + 1)],
                                  in0=pslot[sl],
                                  in1=xrs[i][:, 512 * fc:512 * (fc + 1)],
                                  op=ALU.add),
                             wait=[("tE", mm), xr_tok[i]],
                             inc="vE")
                pfree5[sl] = ("vE", tv)
                if fc == 1:
                    tb = None
                    for ck in range(4):
                        tb = bk.emit(V, "bn_stats",
                                     dict(out=st6[:, 6 * ck:6 * ck + 6],
                                          in_=xrs[i][:, 512 * ck:512 * (ck + 1)]),
                                     wait=([("vE", tv), ln_free["st6"]] if ck == 0
                                           else [("vE", tb)]),
                                     inc="vE")
                    ta = bk.emit(V, "bn_aggr", dict(out=ag2, in_=st6),
                                 wait=[("vE", tb)], inc="vE")
                    tl1 = bk.emit(A, "activation",
                                  dict(out=rr[:, 1:2], in_=ag2[:, 1:2], func=AF.Ln,
                                       bias=cst[:, 3:4]),
                                  wait=[("vE", ta), ln_free["rr"]], inc="aE")
                    tl2 = bk.emit(A, "activation",
                                  dict(out=rr[:, 2:3], in_=rr[:, 1:2], func=AF.Exp,
                                       scale=cst[:, 1:2]),
                                  wait=[("aE", tl1)], inc="aE")
                    ln_free["rr"] = ("aE", tl2)
                    ln_free["st6"] = ("vE", ta)
                    tf = bk.emit(V, "tensor_scalar",
                                 dict(out=xrs[i], in0=xrs[i], scalar1=ag2[:, 0:1],
                                      scalar2=rr[:, 2:3], op0=ALU.subtract,
                                      op1=ALU.mult),
                                 wait=[("aE", tl2)], inc="vE")
                    tfb = bk.emit(V, "tensor_copy",
                                  dict(out=x1bf, in_=xrs[i]),
                                  wait=[("vE", tf), x1bf_free], inc="vE")
                    tsp1 = bk.dma(A, "dAs1", x1sd[128 * tt:128 * (tt + 1), :], xrs[i],
                                  wait=[("vE", tf)])
                    xrs_free[i] = tsp1
                    ttr = bk.dma(SY, "dTx", x1Tst.rearrange("p (t c) -> p t c", t=ET),
                                 x1bf, transpose=True,
                                 wait=[("vE", tfb), x1Tst_free])
                    x1bf_free = ttr
                    tsp2 = bk.dma(A, "dAs2",
                                  x1Tsd[:, :, 128 * tt:128 * (tt + 1)]
                                  .rearrange("t p c -> p t c"),
                                  x1Tst.rearrange("p (t c) -> p t c", t=ET),
                                  wait=[ttr])
                    x1Tst_free = tsp2
            emit_gu_unit()
            emit_gu_unit()
    t_x1T_sp = x1Tst_free

    bk.gate()

    dq["ph6"] = True
    t_x1T = bk.dma(A, "dAxT", x1T.rearrange("p (t c) -> p t c", t=ET),
                   x1Tsd.rearrange("t p c -> p t c"), wait=[t_x1T_sp])
    x1T_r = x1T.rearrange("p (t q) -> p t q", t=ET)
    sil_free = [None, None]
    hstg_free = [None, None]
    pg_free = [None, None]
    pu_free = [None, None]
    hsp_tok = [None] * GU_PAIRS
    for pr in range(GU_PAIRS):
        par = pr % 2
        gf, uf = pr, GU_PAIRS + pr
        pg, pu = ps[2 * par], ps[2 * par + 1]
        mmg = None
        for ht in range(ET):
            for hseg in range(2):
                w = []
                if ht == 0 and hseg == 0:
                    w = [dq["gu_ready"][gf], t_x1T, pg_free[par]]
                mmg = bk.emit(T, "matmul",
                              dict(out=pg[:, 512 * hseg:512 * (hseg + 1)],
                                   lhsT=ring[dq["gu_slot"][gf]][:, 128 * ht:128 * (ht + 1)],
                                   rhs=x1T_r[:, ht, 512 * hseg:512 * (hseg + 1)],
                                   start=(ht == 0), stop=(ht == ET - 1)),
                              wait=w,
                              inc=("tE" if ht == ET - 1 and hseg == 1 else None))
        dq["ring_free"][dq["gu_slot"][gf]] = mmg
        mmu = None
        for ht in range(ET):
            for hseg in range(2):
                w = []
                if ht == 0 and hseg == 0:
                    w = [dq["gu_ready"][uf], pu_free[par]]
                mmu = bk.emit(T, "matmul",
                              dict(out=pu[:, 512 * hseg:512 * (hseg + 1)],
                                   lhsT=ring[dq["gu_slot"][uf]][:, 128 * ht:128 * (ht + 1)],
                                   rhs=x1T_r[:, ht, 512 * hseg:512 * (hseg + 1)],
                                   start=(ht == 0), stop=(ht == ET - 1)),
                              wait=w,
                              inc=("tE" if ht == ET - 1 and hseg == 1 else None))
        dq["ring_free"][dq["gu_slot"][uf]] = mmu
        tsl = bk.emit(A, "activation", dict(out=silwk[par], in_=pg, func=AF.Silu),
                      wait=[("tE", mmg), sil_free[par]], inc="aE")
        pg_free[par] = ("aE", tsl)
        th = bk.emit(V, "tensor_tensor",
                     dict(out=hstg[par], in0=silwk[par], in1=pu, op=ALU.mult),
                     wait=[("aE", tsl), ("tE", mmu), hstg_free[par]], inc="vE")
        pu_free[par] = ("vE", th)
        sil_free[par] = ("vE", th)
        tsp = bk.dma(A, f"dAhs{par}", hTsd[pr], hstg[par], wait=[("vE", th)])
        hstg_free[par] = tsp
        hsp_tok[pr] = tsp
        for _ in range(2):
            if not emit_gu_unit():
                emit_dn_unit()
        emit_dn_unit()
        if pr >= GU_PAIRS - 3:
            emit_dn_unit()
            emit_dn_unit()

    bk.gate()

    chk_tok = [None] * 3
    chk_free = [None] * 3
    chk_n = [0]

    def load_chunk(qq):
        par = chk_n[0] % 3
        chk_n[0] += 1
        w = [chk_free[par], hsp_tok[11 * qq + 10], hsp_tok[11 * qq + 9]]
        cr = chk[par].rearrange("p (i q) -> p i q", i=11)
        hr = hTsd[11 * qq:11 * (qq + 1)].rearrange("i p q -> p i q")
        ta = bk.dma(A, "dAht0", cr[:, 0:6], hr[:, 0:6], wait=w)
        tb = bk.dma(A, "dAht1", cr[:, 6:11], hr[:, 6:11], wait=w)
        chk_tok[par] = (ta, tb)
        return par

    x1r_tok = [None] * 4
    fo_free = [None, None]
    x1rl_started = [False]
    chk1_last = [None]

    def maybe_prefetch_x1rl():
        if x1rl_started[0] or dq["dn_next"] < DN_HT * 4:
            return
        x1rl_started[0] = True
        w = list(dq["dn_pk_free"]) + list(dq["dn_wq_free"])
        x1r_tok[0] = bk.dma(A, "dAxr0", x1rl[0], x1sd[0:128, :], wait=w)
        x1r_tok[1] = bk.dma(A, "dAxr1", x1rl[1], x1sd[128:256, :], wait=w)

    def prefetch_x1rl23():
        x1r_tok[2] = bk.dma(A, "dAxr0", x1rl[2], x1sd[256:384, :],
                            wait=[chk1_last[0]])
        x1r_tok[3] = bk.dma(A, "dAxr1", x1rl[3], x1sd[384:512, :],
                            wait=[chk1_last[0]])

    def emit_out_tile(tt, tev, tstat):
        par = tt % 2
        bpar = tt % 4
        ytile = yres[:, 2048 * tt:2048 * (tt + 1)]
        ta = bk.emit(V, "bn_aggr", dict(out=ag2, in_=st6y[:, 24 * tt:24 * (tt + 1)]),
                     wait=[("vE", tstat), ln_free["st6"]], inc="vE")
        tu = bk.emit(V, "tensor_scalar",
                     dict(out=rr[:, 0:1], in0=ag2[:, 0:1], scalar1=ag2[:, 0:1],
                          scalar2=ag2[:, 1:2], op0=ALU.mult, op1=ALU.add),
                     wait=[("vE", ta), ln_free["rr"]], inc="vE")
        tl1 = bk.emit(A, "activation", dict(out=rr[:, 1:2], in_=rr[:, 0:1], func=AF.Ln,
                                            bias=cst[:, 4:5]),
                      wait=[("vE", tu)], inc="aE")
        tl2 = bk.emit(A, "activation", dict(out=rr[:, 2:3], in_=rr[:, 1:2], func=AF.Exp,
                                            scale=cst[:, 1:2]),
                      wait=[("aE", tl1)], inc="aE")
        tsum = bk.emit(V, "scalar_tensor_tensor",
                       dict(out=outst[par], in0=ytile, scalar=rr[:, 2:3], in1=x1rl[bpar],
                            op0=ALU.mult, op1=ALU.add),
                       wait=[("aE", tl2), x1r_tok[bpar], fo_free[par]], inc="vE")
        if tt + 4 < NTT:
            x1r_tok[bpar] = bk.dma(A, f"dAxr{par}", x1rl[bpar],
                                   x1sd[128 * (tt + 4):128 * (tt + 5), :],
                                   wait=[("vE", tsum)])
        tb2 = None
        for ck in range(4):
            tb2 = bk.emit(V, "bn_stats",
                          dict(out=st6[:, 6 * ck:6 * ck + 6],
                               in_=outst[par][:, 512 * ck:512 * (ck + 1)]),
                          wait=([("vE", tsum)] if ck == 0 else [("vE", tb2)]),
                          inc="vE")
        ta2 = bk.emit(V, "bn_aggr", dict(out=ag2, in_=st6), wait=[("vE", tb2)], inc="vE")
        tl3 = bk.emit(A, "activation", dict(out=rr[:, 1:2], in_=ag2[:, 1:2], func=AF.Ln,
                                            bias=cst[:, 3:4]),
                      wait=[("vE", ta2)], inc="aE")
        tl4 = bk.emit(A, "activation", dict(out=rr[:, 2:3], in_=rr[:, 1:2], func=AF.Exp,
                                            scale=cst[:, 1:2]),
                      wait=[("aE", tl3)], inc="aE")
        ln_free["rr"] = ("aE", tl4)
        ln_free["st6"] = ("vE", ta2)
        tbn = bk.emit(V, "tensor_scalar",
                      dict(out=rr[:, 3:4], in0=ag2[:, 0:1], scalar1=rr[:, 2:3],
                           scalar2=-1.0, op0=ALU.mult, op1=ALU.mult),
                      wait=[("aE", tl4)], inc="vE")
        tout = bk.emit(A, "activation",
                       dict(out=outst[par], in_=outst[par], func=AF.Identity,
                            scale=rr[:, 2:3], bias=rr[:, 3:4]),
                       wait=[("vE", tbn)], inc="aE")
        out_tok = bk.dma(P, f"dPo{par}", outd[128 * tt:128 * (tt + 1), :], outst[par],
                         wait=[("aE", tout)])
        fo_free[par] = out_tok

    y_free = [None] * 8
    load_chunk(0)
    load_chunk(1)
    for c in range(4):
        mml = [None] * 8
        for qq in range(4):
            gtok = dq["dn_ready"][(c, qq, 3)]
            wgr = wdq[(4 * c + qq) % 2].rearrange("p (iq c2) -> p iq c2", iq=11)
            par = (4 * c + qq) % 3
            cbuf = chk[par]
            ctok = chk_tok[par]
            last = (c == 3 and qq == 3)
            order = ([(iq, tt) for tt in range(8) for iq in range(11)] if last
                     else [(iq, tt) for iq in range(11) for tt in range(8)])
            for (iq, tt) in order:
                w = []
                if iq == 0 and (tt == 0 or last):
                    w = [*ctok, gtok]
                if qq == 0 and iq == 0:
                    w = w + (y_free[tt] or [])
                mml[tt] = bk.emit(T, "matmul",
                                  dict(out=pslot[tt],
                                       lhsT=cbuf[:, NQ * iq + 128 * tt:
                                                 NQ * iq + 128 * (tt + 1)],
                                       rhs=wgr[:, iq],
                                       start=(qq == 0 and iq == 0),
                                       stop=(qq == 3 and iq == 10)),
                                  wait=w,
                                  inc=("tE" if (iq == 10 or tt == 7) else None))
                if last and iq == 10:
                    tev = bk.emit(A, "copy",
                                  dict(out=yres[:, 2048 * tt + 512 * c:
                                                2048 * tt + 512 * (c + 1)],
                                       in_=pslot[tt]),
                                  wait=[("tE", mml[tt])], inc="aE")
                    tstat = bk.emit(V, "bn_stats",
                                    dict(out=st6y[:, 24 * tt + 6 * c:24 * tt + 6 * (c + 1)],
                                         in_=pslot[tt]),
                                    wait=[("tE", mml[tt]), ("aE", tev)], inc="vE")
                    emit_out_tile(tt, tev, tstat)
            chk_free[par] = ("tE", mml[7])
            if c == 3 and qq == 1:
                chk1_last[0] = ("tE", mml[7])
                prefetch_x1rl23()
            dq["dn_grp_free"][(4 * c + qq) % 2] = mml[7]
            for _ in range(4):
                emit_dn_unit()
            maybe_prefetch_x1rl()
            if 4 * c + qq <= 13:
                load_chunk((qq + 2) % 4)
        if c < 3:
            for tt in range(8):
                eng2 = A if tt % 2 == 0 else V
                tev = bk.emit(eng2, "copy" if tt % 2 == 0 else "tensor_copy",
                              dict(**({"out": yres[:, 2048 * tt + 512 * c:
                                                   2048 * tt + 512 * (c + 1)],
                                       } |
                                      ({"in_": pslot[tt]}))),
                              wait=[("tE", mml[tt])], inc="aE" if tt % 2 == 0 else "vE")
                tstat = bk.emit(V, "bn_stats",
                                dict(out=st6y[:, 24 * tt + 6 * c:24 * tt + 6 * (c + 1)],
                                     in_=pslot[tt]),
                                wait=[("tE", mml[tt])] +
                                     ([("aE", tev)] if tt % 2 == 0 else []),
                                inc="vE")
                y_free[tt] = [("aE" if tt % 2 == 0 else "vE", tev), ("vE", tstat)]
    bk.emit(P, "wait_only", {}, wait=[fo_free[0], fo_free[1]])

    return nc, bk


def finalize(nc, bk):
    with nc.Block() as blk:
        @blk.tensor
        def _(e):
            bk.replay(e, T)

        @blk.scalar
        def _(e):
            bk.replay(e, A)

        @blk.vector
        def _(e):
            bk.replay(e, V)

        @blk.gpsimd
        def _(e):
            bk.replay(e, P)

        @blk.sync
        def _(e):
            bk.replay(e, SY)
    return nc


def _perm_half(n, chunk=None):
    if chunk is None:
        chunk = n
    p = np.arange(n)
    qq = p // chunk
    w = p % chunk
    half = chunk // 2
    return chunk * qq + np.where(w < half, 2 * w, 2 * (w - half) + 1)


def _host_prep(inputs):
    x = np.asarray(inputs["L0_attn_x"], dtype=np.float32)
    Wqkv = np.asarray(inputs["L0_attn_Wqkv"], dtype=np.float32)
    bqkv = np.asarray(inputs["L0_attn_bqkv"], dtype=np.float32)
    Wout = np.asarray(inputs["L0_attn_Wout"], dtype=np.float32)
    bout = np.asarray(inputs["L0_attn_bout"], dtype=np.float32)
    gup = np.ascontiguousarray(np.asarray(inputs["L0_ffn_gate_up_packed"], dtype=np.int32))
    gusc = np.asarray(inputs["L0_ffn_gate_up_scales"])
    dnp = np.ascontiguousarray(np.asarray(inputs["L0_ffn_down_packed"], dtype=np.int32))
    dnsc = np.asarray(inputs["L0_ffn_down_scales"])

    Pgu = _perm_half(E)
    Pdn = _perm_half(IFF, 1408)

    inv = (10000.0 ** (-(np.arange(0, E, 2, dtype=np.float32) / E))).astype(np.float32)
    emb = np.repeat(inv, 2)
    ang = np.float32(S) * emb
    sin = np.sin(ang, dtype=np.float32)
    cos = np.cos(ang, dtype=np.float32)
    sinS = np.concatenate([-sin[0:E // 2], sin[E // 2:]]).astype(np.float32)
    cosR = np.ascontiguousarray(np.broadcast_to(cos, (128, E)))
    sinSR = np.ascontiguousarray(np.broadcast_to(sinS, (128, E)))

    WqkvT = np.ascontiguousarray(Wqkv[:, Pgu].T)
    WoutT = np.ascontiguousarray(Wout[Pgu, :].T)
    bqkvR = np.ascontiguousarray(bqkv[None, :])
    boutR = np.ascontiguousarray(bout[Pgu][None, :])

    gup_dev = np.ascontiguousarray(
        np.concatenate([gup[0:IFF][Pdn], gup[IFF:][Pdn]], axis=0))
    gus_sm = np.ascontiguousarray(
        np.concatenate([gusc.T[0:IFF][Pdn], gusc.T[IFF:][Pdn]], axis=0)).astype(np.float16)
    dnp_dev = np.ascontiguousarray(dnp[Pgu])
    dns_sm = np.ascontiguousarray(dnsc.T[Pgu]).astype(np.float16)
    cg = np.arange(2048)
    gus_dev = np.ascontiguousarray(gus_sm[:, (cg % 1024) // 64])
    cd = np.arange(IFF)
    dns_dev = np.ascontiguousarray(dns_sm[:, (cd // 1408) * 11 + (cd % 704) // 64])

    in_maps = []
    for c in range(8):
        b, r = c // 2, c % 2
        xd = x[b][:, Pgu]
        xT = np.ascontiguousarray(xd.T)
        xTq = np.ascontiguousarray(xT[:, r::2])
        xres = np.ascontiguousarray(xd[r::2, :])
        kl = np.arange(128)[:, None]
        ql = np.arange(128)[None, :]
        maskAB = np.ascontiguousarray(np.concatenate([
            (kl <= 2 * ql + r), (kl + 128 <= 2 * ql + r)],
            axis=1).astype(ml_dtypes.bfloat16))
        in_maps.append(dict(
            xT=xT, xTq=xTq, xres=xres, wqkvT=WqkvT, bqkvR=bqkvR, woutT=WoutT,
            boutR=boutR, cosR=cosR, sinSR=sinSR, maskAB=maskAB,
            gup=gup_dev, gus=gus_dev, dnp=dnp_dev, dns=dns_dev))
    return in_maps, Pgu


_CACHE = {}


def kernel(**inputs):
    in_maps, Pgu = _host_prep(inputs)
    if "nc" not in _CACHE:
        nc, bk = build_program()
        finalize(nc, bk)
        _CACHE["nc"] = nc
    nc = _CACHE["nc"]
    res = run_bass_kernel_spmd(nc, in_maps, core_ids=list(range(8)))
    out = np.empty((4, S, E), dtype=np.float32)
    for c in range(8):
        b, r = c // 2, c % 2
        oc = res.results[c]["out"]
        out[b, r::2][:, Pgu] = oc
    return out


# version 18
# speedup vs baseline: 1.0182x; 1.0182x over previous
import numpy as np
import ml_dtypes
import concourse.bass as bass
import concourse.mybir as mybir
from concourse.bass_utils import run_bass_kernel_spmd
from concourse.alu_op_type import AluOpType as ALU

F32 = mybir.dt.float32
F16 = mybir.dt.float16
BF16 = mybir.dt.bfloat16
I32 = mybir.dt.int32
I16 = mybir.dt.int16
U16 = mybir.dt.uint16
FP8 = mybir.dt.float8e4
AF = mybir.ActivationFunctionType

S, E, H = 2048, 2048, 16
IFF = 5632
FGU = 2 * IFF
NQ = 1024
NTT, NST, ET = 8, 16, 16
GU_FT, GU_PAIRS, DN_HT, DN_IT = 88, 44, 16, 44
KB = 1024

ENGINES = ("T", "A", "V", "P", "S")
T, A, V, P, SY = ENGINES


class BK:
    def __init__(self, nc):
        self.nc = nc
        self.sem = {}
        self.cnt = {}
        self.ops = {e: [] for e in ENGINES}
        self.mark = {e: {} for e in ENGINES}

    def s(self, name):
        if name not in self.sem:
            self.sem[name] = self.nc.alloc_semaphore(name)
            self.cnt[name] = 0
        return self.sem[name]

    def emit(self, eng, fname, kwargs=None, wait=(), inc=None, incby=1):
        if inc:
            self.s(inc)
        ws = []
        for w in wait:
            if w is None:
                continue
            sn, v = w
            if v is None:
                continue
            self.s(sn)
            if self.mark[eng].get(sn, 0) >= v:
                continue
            self.mark[eng][sn] = v
            ws.append((sn, v))
        tok = None
        if inc is not None:
            self.cnt[inc] += incby
            tok = self.cnt[inc]
        self.ops[eng].append((fname, kwargs or {}, ws, inc, incby))
        return tok

    def dma(self, eng, semname, out, in_, wait=(), **kw):
        wait = list(wait)
        self.s(semname)
        if self.cnt[semname]:
            wait.append((semname, self.cnt[semname]))
        tok = self.emit(eng, "dma_start", dict(out=out, in_=in_, **kw),
                        wait=wait, inc=semname, incby=16)
        return (semname, tok)

    def gate(self):
        snap = [(sn, v) for sn, v in self.cnt.items()]
        for e in ENGINES:
            self.emit(e, "wait_only", {}, wait=snap)

    def replay(self, eng_obj, eng):
        for fname, kwargs, ws, inc, incby in self.ops[eng]:
            for sn, v in ws:
                eng_obj.wait_ge(self.sem[sn], v)
            if fname == "wait_only":
                continue
            r = getattr(eng_obj, fname)(**kwargs)
            if inc is not None:
                r.then_inc(self.sem[inc], incby)


def build_program():
    nc = bass.Bass("TRN2", target_bir_lowering=False)

    def din(name, shape, dt):
        return nc.dram_tensor(name, shape, dt, kind="ExternalInput").ap()

    xTd = din("xT", [E, S], F32)
    xTqd = din("xTq", [E, NQ], F32)
    xresd = din("xres", [NQ, E], F32)
    wqkvT = din("wqkvT", [E, 3 * E], F32)
    bqkvR = din("bqkvR", [1, 3 * E], F32)
    woutTd = din("woutT", [E, E], F32)
    boutR = din("boutR", [1, E], F32)
    cosRd = din("cosR", [128, E], F32)
    sinRd = din("sinSR", [128, E], F32)
    maskABd = din("maskAB", [128, 256], BF16)
    gup = din("gup", [FGU, E // 8], I32)
    gusX = din("gus", [FGU, 2048], F16)
    dnp = din("dnp", [E, IFF // 8], I32)
    dnsX = din("dns", [E, IFF], F16)
    outd = nc.dram_tensor("out", [NQ, E], F32, kind="ExternalOutput").ap()

    kTsd = nc.dram_tensor("kTs", [H, 128, S], BF16, kind="Internal").ap()
    vsd = nc.dram_tensor("vsd", [H, S, 128], BF16, kind="Internal").ap()
    x1sd = nc.dram_tensor("x1s", [NQ, E], F32, kind="Internal").ap()
    x1Tsd = nc.dram_tensor("x1Ts", [ET, 128, NQ], BF16, kind="Internal").ap()
    hTsd = nc.dram_tensor("hTs", [DN_IT, 128, NQ], BF16, kind="Internal").ap()

    MANUAL_SHIFT_KB = 12
    def at(name, off_kb, shape, dt):
        return nc.alloc_sbuf_tensor_at(name, shape, dt,
                                       offset=int((off_kb + MANUAL_SHIFT_KB) * KB)).ap()

    ones = nc.alloc_sbuf_tensor("ones", [128, 128], BF16).ap()
    cst = nc.alloc_sbuf_tensor("cst", [128, 8], F32).ap()
    maskA = nc.alloc_sbuf_tensor("maskA", [128, 128], BF16).ap()
    maskB = nc.alloc_sbuf_tensor("maskB", [128, 128], BF16).ap()
    st6 = nc.alloc_sbuf_tensor("st6", [128, 24], F32).ap()
    ag2 = nc.alloc_sbuf_tensor("ag2", [128, 2], F32).ap()
    rr = nc.alloc_sbuf_tensor("rr", [128, 4], F32).ap()
    st6y = nc.alloc_sbuf_tensor("st6y", [128, 8 * 24], F32).ap()

    cosSB = at("cosSB", 16, [128, E], F32)
    sinSB = at("sinSB", 24, [128, E], F32)
    xrs = [at("xrs0", 16, [128, E], F32), at("xrs1", 24, [128, E], F32),
           at("xrs2", 32, [128, E], F32), at("xrs3", 40, [128, E], F32)]
    dnf8 = at("dnf8", 16, [128, 352], I32)
    dnsb = at("dnsb", 17.5, [128, 352], I32)
    dnpk = at("dnpk", 19, [128, 176], I32)
    dnA = at("dnA", 20, [128, 1408], BF16)
    dnWQ = [at("dnWQ0", 29, [128, 1408], BF16), at("dnWQ1", 26, [128, 1408], BF16)]
    x1rl = [at("x1rl0", 16, [128, 2048], F32), at("x1rl1", 24, [128, 2048], F32),
            at("x1rl2", 32, [128, 2048], F32), at("x1rl3", 40, [128, 2048], F32)]

    stg = [at(f"stg{i}", 40 + 4 * i, [128, ET * 64], F32) for i in range(8)]
    ewk0 = at("ewk0", 32, [128, E], F32)
    ewk1 = at("ewk1", 40, [128, E], F32)
    kstg = at("kstg", 48, [128, E], BF16)
    ktr = at("ktr", 52, [128, E], BF16)
    ewk0b = at("ewk0b", 56, [128, E], F32)
    xq = [at(f"xq{i}", 64 + 4 * i, [128, ET * 128], BF16) for i in range(2)]
    vstg = [at(f"vstg{i}", 52 + i, [128, 512], BF16) for i in range(4)]
    kh = [at("kh0", 32, [128, S], BF16), at("kh1", 36, [128, S], BF16)]
    vh = [at("vh0", 40, [128, H * 128], BF16), at("vh1", 44, [128, H * 128], BF16)]
    mex = [at(f"mex{i}", (48, 50, 56, 58)[i], [128, NQ], BF16) for i in range(4)]
    zrS = at("zrS", 52, [128, NQ], F32)
    x1bf = at("x1bf", 48, [128, E], BF16)
    x1Tst = at("x1Tst", 52, [128, E], BF16)
    silwk = [at("silwk0", 32, [128, NQ], F32), at("silwk1", 36, [128, NQ], F32)]
    hstg = [at("hstg0", 40, [128, NQ], BF16), at("hstg1", 42, [128, NQ], BF16)]
    chk = [at("chk0", 112, [128, 11 * NQ], BF16), at("chk1", 32, [128, 11 * NQ], BF16),
           at("chk2", 54, [128, 11 * NQ], BF16)]
    yres = at("yres", 140, [128, 8 * 2048], F32)
    outst = [at("outst0", 54, [128, 2048], F32), at("outst1", 62, [128, 2048], F32)]

    gsc = [at("gsc0", 204, [128, 2048], F16), at("gsc1", 98, [128, 2048], F16)]
    dsc = [at("dsc0", 102, [128, 1408], F16), at("dsc1", 105, [128, 1408], F16)]
    guf8 = at("guf8", 57, [128, 512], I32)
    gusb = at("gusb", 59, [128, 512], I32)
    guA = at("guA", 61, [128, 2048], BF16)
    guWQ = [at("guWQ0", 65, [128, 2048], BF16), at("guWQ1", 44, [128, 2048], BF16)]
    gupk = at("gupk", 56, [128, 256], I32)
    bq = [at(f"bq{j}", 72 + j, [1, 512], BF16) for j in range(4)]

    wres = at("wres", 76, [128, 4 * ET * 512], BF16)
    avn = at("avn", 76, [128, H * NQ], BF16)
    wocol = [at("wocol0", 140, [128, ET * 512], BF16),
             at("wocol1", 156, [128, ET * 512], BF16),
             at("wocol2", 108, [128, ET * 512], BF16),
             at("wocol3", 124, [128, ET * 512], BF16)]
    qT = at("qT", 140, [128, H * NQ], BF16)
    xTb = at("xTb", 140, [128, ET * S], BF16)
    x1T = at("x1T", 140, [128, NTT * E], BF16)

    NRING = 8
    ring = [at(f"ring{i}", 172 + 4 * i, [128, E], BF16) for i in range(NRING)]
    wdq = [at("wdq0", 76, [128, 4 * 1408], BF16), at("wdq1", 87, [128, 4 * 1408], BF16)]

    ps = [nc.alloc_psum_tensor(f"ps{i}", [128, 1024], F32).ap() for i in range(4)]
    pslot = [ps[i // 2][:, (i % 2) * 512:(i % 2) * 512 + 512] for i in range(8)]

    bk = BK(nc)

    bk.dma(SY, "dCst", maskA, maskABd[:, 0:128])
    t_msk = bk.dma(SY, "dCst", maskB, maskABd[:, 128:256])
    t_cos = bk.dma(SY, "dCst", cosSB, cosRd[:])
    t_sin = bk.dma(SY, "dCst", sinSB, sinRd[:])
    for col, val in [(0, 1e-24), (1, -0.5), (2, float(np.log(2048.0 ** -0.25))),
                     (3, 1e-5), (4, 1e-6), (5, -1.0), (6, 64.0)]:
        bk.emit(V, "memset", dict(ap=cst[:, col:col + 1], constant=val), inc="vE")
    t_cst = bk.emit(V, "memset", dict(ap=ones, constant=1.0), inc="vE")

    dq = dict(gu_next=0, dn_next=0,
              gu_ready=[None] * GU_FT, gu_slot=[None] * GU_FT,
              ring_free=[None] * NRING,
              gu_pk_free=[], gu_wq_free=[None, None],
              dn_ready={}, dn_grp_free=[None, None],
              dn_pk_free=[], dn_wq_free=[None, None],
              gu_a_free=None, dn_a_free=None,
              gu_sc_free=[None, None], dn_sc_free=[None, None],
              ph6=False)

    def gu_order(i):
        return (i // 2) if i % 2 == 0 else GU_PAIRS + (i // 2)

    def _dq_pipe(pk, nw, f8, sbb, Ab, WQ, s_ap, t_sc, t_load, wq_free, a_free):
        nb = 4 * nw
        t1 = bk.emit(V, "tensor_scalar",
                     dict(out=f8[:, 0:nw], in0=pk, scalar1=2, scalar2=0x1C1C1C1C,
                          op0=ALU.logical_shift_left, op1=ALU.bitwise_and),
                     wait=[t_load], inc="vE")
        t1b = bk.emit(V, "tensor_scalar",
                      dict(out=f8[:, nw:2 * nw], in0=pk, scalar1=2, scalar2=0x1C1C1C1C,
                           op0=ALU.logical_shift_right, op1=ALU.bitwise_and),
                      wait=[("vE", t1)], inc="vE")
        t2 = bk.emit(V, "tensor_scalar",
                     dict(out=sbb[:, 0:nw], in0=pk, scalar1=0x08080808, scalar2=4,
                          op0=ALU.bitwise_and, op1=ALU.logical_shift_left),
                     wait=[t_load], inc="vE")
        t3 = bk.emit(V, "tensor_scalar",
                     dict(out=sbb[:, nw:2 * nw], in0=pk, scalar1=4, scalar2=0x08080808,
                          op0=ALU.logical_shift_right, op1=ALU.bitwise_and),
                     wait=[("vE", t2)], inc="vE")
        t4 = bk.emit(V, "tensor_scalar",
                     dict(out=sbb[:, nw:2 * nw], in0=sbb[:, nw:2 * nw], scalar1=4,
                          scalar2=None, op0=ALU.logical_shift_left),
                     wait=[("vE", t3)], inc="vE")
        t6 = bk.emit(V, "tensor_tensor",
                     dict(out=f8, in0=f8, in1=sbb, op=ALU.bitwise_or),
                     wait=[("vE", t1b), ("vE", t4)], inc="vE")
        t7 = bk.emit(A, "activation",
                     dict(out=Ab, in_=f8.bitcast(FP8), func=AF.Identity,
                          scale=cst[:, 6:7]),
                     wait=[("vE", t6), a_free, ("vE", t_cst)], inc="aE")
        t12 = bk.emit(V, "tensor_tensor",
                      dict(out=WQ, in0=Ab, in1=s_ap, op=ALU.mult),
                      wait=[("aE", t7), wq_free, t_sc], inc="vE")
        wq_toks = [("vE", t12)]
        return [("vE", t3)], wq_toks, ("vE", t12)

    def emit_gu_unit():
        i = dq["gu_next"]
        if i >= GU_FT:
            return False
        slot = i % NRING
        if i >= NRING and dq["ring_free"][slot] is None:
            return False
        dq["gu_next"] = i + 1
        ft = gu_order(i)
        wb = (i % 2) if dq["ph6"] else 0
        tl = bk.dma(P, "dSg", gupk, gup[128 * ft:128 * (ft + 1), :],
                    wait=dq["gu_pk_free"])
        tsc_ld = bk.dma(A, f"dScg{wb}", gsc[wb], gusX[128 * ft:128 * (ft + 1), :],
                        wait=[dq["gu_sc_free"][wb]])
        pk_free, wq, af = _dq_pipe(gupk, 256, guf8, gusb, guA, guWQ[wb],
                                   gsc[wb], tsc_ld, tl,
                                   dq["gu_wq_free"][wb], dq["gu_a_free"])
        dq["gu_a_free"] = af
        dq["gu_sc_free"][wb] = wq[0]
        tt = bk.dma(SY, f"dTr{slot}", ring[slot].rearrange("p (t c) -> p t c", t=ET),
                    guWQ[wb], transpose=True,
                    wait=wq + [("tE", dq["ring_free"][slot])])
        dq["gu_pk_free"] = pk_free
        dq["gu_wq_free"][wb] = tt
        dq["gu_ready"][ft] = tt
        dq["gu_slot"][ft] = slot
        return True

    def emit_dn_unit():
        i = dq["dn_next"]
        if i >= DN_HT * 4:
            return False
        if i >= 8 and dq["dn_grp_free"][(i // 4) % 2] is None:
            return False
        dq["dn_next"] = i + 1
        c, qq, j = i // 16, (i // 4) % 4, i % 4
        ht = 4 * c + j
        grp = (i // 4) % 2
        wb = i % 2
        tl = bk.dma(P, "dSd", dnpk, dnp[128 * ht:128 * (ht + 1), 176 * qq:176 * (qq + 1)],
                    wait=dq["dn_pk_free"])
        tsc_ld = bk.dma(A, f"dScd{wb}", dsc[wb],
                        dnsX[128 * ht:128 * (ht + 1), 1408 * qq:1408 * (qq + 1)],
                        wait=[dq["dn_sc_free"][wb]])
        pk_free, wq, af = _dq_pipe(dnpk, 176, dnf8, dnsb, dnA, dnWQ[wb],
                                   dsc[wb], tsc_ld, tl,
                                   dq["dn_wq_free"][wb], dq["dn_a_free"])
        dq["dn_a_free"] = af
        dq["dn_sc_free"][wb] = wq[0]
        dst = wdq[grp].rearrange("p (t j c) -> p t j c", t=11, j=4)[:, :, j]
        tt = bk.dma(SY, f"dTw{grp}", dst, dnWQ[wb], transpose=True,
                    wait=wq + [("tE", dq["dn_grp_free"][grp])])
        dq["dn_pk_free"] = pk_free
        dq["dn_wq_free"][wb] = tt
        dq["dn_ready"][(c, qq, j)] = tt
        return True

    xT_r = xTb.rearrange("p (et s) -> p et s", et=ET)
    xTd_r = xTd.rearrange("(et p) s -> p et s", p=128)
    wres_r = wres.rearrange("p (j et f) -> p j et f", j=4, et=ET)

    def load_col(dst4, srcd, g, wait=()):
        toks = []
        for c in range(4):
            srcc = (srcd[:, 512 * g + 128 * c:512 * g + 128 * (c + 1)]
                    .rearrange("(et p) f -> p et f", p=128))
            toks.append(bk.dma(P, f"dPk{c}", dst4[:, :, 128 * c:128 * (c + 1)],
                               srcc, wait=wait))
        return toks

    kcol_tok = [None] * 4
    kbq_tok = [None] * 4
    stg_cast = [None] * 8
    for j in range(4):
        tc = None
        for e in range(8):
            srce = (wqkvT[:, 512 * (4 + j) + 64 * e:512 * (4 + j) + 64 * (e + 1)]
                    .rearrange("(et p) f -> p et f", p=128))
            tl = bk.dma(SY, f"dSw{e}", stg[e].rearrange("p (et f) -> p et f", et=ET),
                        srce, wait=[stg_cast[e]])
            tc = bk.emit(A, "copy",
                         dict(out=wres_r[:, j, :, 64 * e:64 * (e + 1)],
                              in_=stg[e].rearrange("p (et f) -> p et f", et=ET)),
                         wait=[tl], inc="aE")
            stg_cast[e] = ("aE", tc)
        kcol_tok[j] = [("aE", tc)]
        kbq_tok[j] = bk.dma(P, "dBq", bq[j], bqkvR[0:1, 512 * (4 + j):512 * (5 + j)])
    stg_done = ("aE", tc)

    t_xTt = []
    for st in range(NST):
        ta = bk.dma(P, f"dP{(2 * st) % 4}", xT_r[:, 0:8, 128 * st:128 * (st + 1)],
                    xTd_r[:, 0:8, 128 * st:128 * (st + 1)])
        tb = bk.dma(P, f"dP{(2 * st + 1) % 4}", xT_r[:, 8:16, 128 * st:128 * (st + 1)],
                    xTd_r[:, 8:16, 128 * st:128 * (st + 1)])
        t_xTt.append((ta, tb))

    pfree = [None] * 8
    ewkbufs = [ewk0, ewk0b]
    rn = dict(ewk_free=[None, stg_done], g_free=stg_done, rr_free=None,
              stage_free=stg_done, st6_free=None)

    def proj_chains(st, lhs_fn, rhs_fn, w0_fn):
        evac_last = None
        mm_fc = [None] * 4
        ewkP = ewkbufs[st % 2]
        for fc in range(4):
            sl = (st % 2) * 4 + fc
            w0 = list(w0_fn(fc)) + [pfree[sl]]
            bk.emit(T, "matmul",
                    dict(out=pslot[sl], lhsT=ones[0:1, :],
                         rhs=bq[fc][0:1, :],
                         start=True, stop=False), wait=w0)
            mm = None
            for et in range(ET):
                mm = bk.emit(T, "matmul",
                             dict(out=pslot[sl], lhsT=lhs_fn(et),
                                  rhs=rhs_fn(fc, et),
                                  start=False, stop=(et == ET - 1)),
                             inc=("tE" if et == ET - 1 else None))
            mm_fc[fc] = mm
            evac_last = bk.emit(A, "copy",
                                dict(out=ewkP[:, 512 * fc:512 * (fc + 1)], in_=pslot[sl]),
                                wait=[("tE", mm)] +
                                     ([rn["ewk_free"][st % 2]] if fc == 0 else []),
                                inc="aE")
            pfree[sl] = ("aE", evac_last)
        return evac_last, mm_fc

    def rope_l2(st, evac_last, stage):
        ewkP = ewkbufs[st % 2]
        tg1 = bk.emit(V, "tensor_copy", dict(out=ewk1[:, 0:1024], in_=ewkP[:, 1:2048:2]),
                      wait=[("aE", evac_last), rn["g_free"]], inc="vE")
        tg2 = bk.emit(V, "tensor_copy", dict(out=ewk1[:, 1024:2048], in_=ewkP[:, 0:2048:2]),
                      wait=[("vE", tg1)], inc="vE")
        tg3 = bk.emit(V, "tensor_tensor", dict(out=ewk1, in0=ewk1, in1=sinSB, op=ALU.mult),
                      wait=[("vE", tg2), t_sin], inc="vE")
        tx1 = bk.emit(V, "tensor_tensor", dict(out=ewkP, in0=ewkP, in1=cosSB, op=ALU.mult),
                      wait=[("vE", tg2), t_cos], inc="vE")
        tro = bk.emit(V, "tensor_tensor", dict(out=ewkP, in0=ewkP, in1=ewk1, op=ALU.add),
                      wait=[("vE", tg3), ("vE", tx1)], inc="vE")
        rn["g_free"] = ("vE", tro)
        tb = None
        for ck in range(4):
            tb = bk.emit(V, "bn_stats",
                         dict(out=st6[:, 6 * ck:6 * ck + 6],
                              in_=ewkP[:, 512 * ck:512 * (ck + 1)]),
                         wait=([("vE", tro), rn["st6_free"]] if ck == 0 else [("vE", tb)]),
                         inc="vE")
        ta = bk.emit(V, "bn_aggr", dict(out=ag2, in_=st6), wait=[("vE", tb)], inc="vE")
        tu = bk.emit(V, "tensor_scalar",
                     dict(out=rr[:, 0:1], in0=ag2[:, 0:1], scalar1=ag2[:, 0:1],
                          scalar2=ag2[:, 1:2], op0=ALU.mult, op1=ALU.add),
                     wait=[("vE", ta), rn["rr_free"]], inc="vE")
        tu2 = bk.emit(V, "tensor_scalar",
                      dict(out=rr[:, 0:1], in0=rr[:, 0:1], scalar1=2048.0,
                           scalar2=cst[:, 0:1], op0=ALU.mult, op1=ALU.max),
                      wait=[("vE", tu)], inc="vE")
        tl1 = bk.emit(A, "activation", dict(out=rr[:, 1:2], in_=rr[:, 0:1], func=AF.Ln),
                      wait=[("vE", tu2)], inc="aE")
        tl2 = bk.emit(A, "activation",
                      dict(out=rr[:, 2:3], in_=rr[:, 1:2], func=AF.Exp,
                           scale=cst[:, 1:2], bias=cst[:, 2:3]),
                      wait=[("aE", tl1)], inc="aE")
        rn["rr_free"] = ("aE", tl2)
        tsc = bk.emit(V, "tensor_scalar",
                      dict(out=stage, in0=ewkP, scalar1=rr[:, 2:3], scalar2=None,
                           op0=ALU.mult),
                      wait=[("aE", tl2), rn["stage_free"]], inc="vE")
        rn["ewk_free"][st % 2] = ("vE", tsc)
        rn["st6_free"] = ("vE", tu2)
        return tsc

    k_last = [None] * 4
    kpendq = []
    ktr_free = [None]

    def finish_k_tile():
        if not kpendq:
            return
        st, ev = kpendq.pop(0)
        tsc = rope_l2(st, ev, kstg)
        ttr = bk.dma(SY, "dTk", ktr.rearrange("p (t c) -> p t c", t=ET), kstg,
                     transpose=True, wait=[("vE", tsc), ktr_free[0]])
        rn["stage_free"] = ttr
        tsp = bk.dma(A, "dAk",
                     kTsd[:, :, 128 * st:128 * (st + 1)].rearrange("h p c -> p h c"),
                     ktr.rearrange("p (h c) -> p h c", h=H),
                     wait=[ttr])
        ktr_free[0] = tsp

    for st in range(NST):
        ev, mfc = proj_chains(
            st,
            lambda et, st=st: xT_r[:, et, 128 * st:128 * (st + 1)],
            lambda fc, et: wres_r[:, fc, et],
            lambda fc, st=st: kcol_tok[fc] + [kbq_tok[fc], *t_xTt[st], ("vE", t_cst)])
        k_last = mfc
        finish_k_tile()
        kpendq.append((st, ev))
    finish_k_tile()

    vcol_tok = [None] * 4
    vbq_tok = [None] * 4
    vstg_free = [None] * 4
    v_last = [None] * 4
    ch = 0
    for fc in range(4):
        vcol_tok[fc] = load_col(wres_r[:, fc], wqkvT, 8 + fc,
                                wait=[("tE", k_last[fc])])
        vbq_tok[fc] = bk.dma(P, "dBq", bq[fc],
                             bqkvR[0:1, 512 * (8 + fc):512 * (9 + fc)],
                             wait=[("tE", k_last[fc])])
        mm = None
        for st in range(NST):
            sl = ch % 8
            ch += 1
            w0 = vcol_tok[fc] + [vbq_tok[fc], *t_xTt[st], pfree[sl], ("vE", t_cst)]
            bk.emit(T, "matmul",
                    dict(out=pslot[sl], lhsT=ones[0:1, :],
                         rhs=bq[fc][0:1, :],
                         start=True, stop=False), wait=w0)
            mm = None
            for et in range(ET):
                mm = bk.emit(T, "matmul",
                             dict(out=pslot[sl], lhsT=xT_r[:, et, 128 * st:128 * (st + 1)],
                                  rhs=wres_r[:, fc, et],
                                  start=False, stop=(et == ET - 1)),
                             inc=("tE" if et == ET - 1 else None))
            par = st % 4
            wv = [("tE", mm), vstg_free[par]]
            if fc == 0 and st < 4:
                wv.append(ktr_free[0])
            te = bk.emit(A, "copy", dict(out=vstg[par], in_=pslot[sl]),
                         wait=wv, inc="aE")
            pfree[sl] = ("aE", te)
            tsp = bk.dma(A, f"dAv{par}",
                         vsd[4 * fc:4 * (fc + 1), 128 * st:128 * (st + 1), :]
                         .rearrange("h s d -> s h d"),
                         vstg[par].rearrange("p (h d) -> p h d", h=4),
                         wait=[("aE", te)])
            vstg_free[par] = tsp
        v_last[fc] = mm

    qcol_tok = [None] * 4
    qbq_tok = [None] * 4
    for j in range(4):
        qcol_tok[j] = load_col(wres_r[:, j], wqkvT, j,
                               wait=[("tE", v_last[j])])
        qbq_tok[j] = bk.dma(P, "dBq", bq[j], bqkvR[0:1, 512 * j:512 * (j + 1)],
                            wait=[("tE", v_last[j])])

    xq_tok = [None] * NTT
    xq_free = [None, None]
    xTqd_r = xTqd.rearrange("(et p) t -> p et t", p=128)

    def load_xq(tt):
        b = tt % 2
        xqr = xq[b].rearrange("p (et c) -> p et c", et=ET)
        w = [xq_free[b], stg_done]
        ta = bk.dma(P, "dPv0", xqr[:, 0:8], xTqd_r[:, 0:8, 128 * tt:128 * (tt + 1)],
                    wait=w)
        tb = bk.dma(P, "dPv1", xqr[:, 8:16], xTqd_r[:, 8:16, 128 * tt:128 * (tt + 1)],
                    wait=w)
        xq_tok[tt] = (ta, tb)

    for tt in range(2):
        load_xq(tt)
    qT_h = qT.rearrange("p (h q) -> p h q", h=H)
    t_qlast = [None]
    q_last = [None] * 4
    qpend = [None]

    def finish_q_tile():
        if qpend[0] is None:
            return
        tt, ev = qpend[0]
        qpend[0] = None
        tsc = rope_l2(tt, ev, kstg)
        ttr = bk.dma(SY, "dTq", qT_h[:, :, 128 * tt:128 * (tt + 1)], kstg,
                     transpose=True,
                     wait=[("vE", tsc)] + ([("tE", v_last[3])] if tt == 0 else []))
        rn["stage_free"] = ttr
        t_qlast[0] = ttr

    for tt in range(NTT):
        ev, mfc = proj_chains(
            tt,
            lambda et, b=tt % 2: xq[b][:, 128 * et:128 * (et + 1)],
            lambda fc, et: wres_r[:, fc, et],
            lambda fc, tt=tt: qcol_tok[fc] + [qbq_tok[fc], *xq_tok[tt]])
        q_last = mfc
        xq_free[tt % 2] = ("tE", mfc[3])
        if tt + 2 < NTT:
            load_xq(tt + 2)
        finish_q_tile()
        qpend[0] = (tt, ev)
    finish_q_tile()
    t_qlast = t_qlast[0]

    bk.gate()

    wocol_r = [wocol[fc].rearrange("p (et f) -> p et f", et=ET) for fc in range(4)]
    wo_tok = [None] * 4
    wbq_tok = [None] * 4
    for fc in (2, 3):
        wo_tok[fc] = load_col(wocol_r[fc], woutTd, fc, wait=[("tE", q_last[fc])])
    for fc in range(4):
        wbq_tok[fc] = bk.dma(P, "dBq", bq[fc], boutR[0:1, 512 * fc:512 * (fc + 1)],
                             wait=[("tE", q_last[fc])])

    kh_tok = [None, None]
    vh_tok = [None, None]
    kh_free = [None, None]
    vh_free = [None, None]

    def prefetch_head(h):
        par = h % 2
        kh_tok[par] = bk.dma(A, f"dAh{par}", kh[par], kTsd[h], wait=[kh_free[par]])
        vh_tok[par] = bk.dma(A, f"dAvh{par}", vh[par].rearrange("p (t d) -> p t d", t=NST),
                             vsd[h].rearrange("(t p) d -> p t d", p=128),
                             wait=[vh_free[par]])

    prefetch_head(0)
    prefetch_head(1)

    av_free = None
    zr_free = None
    sc_free = [None, None]
    mex_free = [None] * 4

    def segs_for(kt):
        voff = 128 * (kt // 2)
        segs = []
        if kt % 2 == 1:
            segs.append((voff, 128, True))
            lo = voff + 128
        else:
            lo = voff
        if kt == NST - 1:
            return voff, segs
        if lo < 512:
            segs.append((lo, 512 - lo, False))
            segs.append((512, 512, False))
        elif lo < NQ:
            segs.append((lo, NQ - lo, False))
        return voff, segs

    for h in range(H):
        par = h % 2
        sc_tok = {}
        exp_tok = {}
        msk_tok = {}
        av_last = None
        z_last = None
        for step in range(NST + 3):
            if step < NST:
                kt = step
                p_sc = ps[kt % 2]
                voff = 128 * (kt // 2)
                segs = ([(voff, 512 - voff), (512, 512)] if voff < 512
                        else [(voff, NQ - voff)])
                mm = None
                for (o, n) in segs:
                    rhs = qT[:, NQ * h + o:NQ * h + o + n]
                    mm = bk.emit(T, "matmul",
                                 dict(out=p_sc[:, o:o + n],
                                      lhsT=kh[par][:, 128 * kt:128 * (kt + 1)],
                                      rhs=rhs, start=True, stop=True),
                                 wait=[kh_tok[par], t_qlast,
                                       sc_free[kt % 2]] if (o, n) == segs[0] else (),
                                 inc=("tE" if (o, n) == segs[-1] else None))
                sc_tok[kt] = mm
                et_ = bk.emit(A, "activation",
                              dict(out=mex[kt % 4][:, voff:NQ], in_=p_sc[:, voff:NQ],
                                   func=AF.Exp),
                              wait=[("tE", mm), mex_free[kt % 4]], inc="aE")
                exp_tok[kt] = et_
                sc_free[kt % 2] = ("aE", et_)
                mk = maskA if kt % 2 == 0 else maskB
                mt = bk.emit(V, "tensor_tensor",
                             dict(out=mex[kt % 4][:, voff:voff + 128],
                                  in0=mex[kt % 4][:, voff:voff + 128],
                                  in1=mk, op=ALU.mult),
                             wait=[("aE", et_), t_msk], inc="vE")
                msk_tok[kt] = mt
            if step >= 3:
                kt = step - 3
                voff, segs = segs_for(kt)
                first = (kt == 0)
                lastk = (kt == NST - 1)
                for (o, n, stp) in segs:
                    w = [("vE", msk_tok[kt]), vh_tok[par]]
                    if first and (o, n, stp) == segs[0]:
                        w += [av_free]
                    av_last = bk.emit(T, "matmul",
                                      dict(out=ps[2][:, o:o + n],
                                           lhsT=vh[par][:, 128 * kt:128 * (kt + 1)],
                                           rhs=mex[kt % 4][:, o:o + n],
                                           start=first, stop=stp,
                                           skip_group_check=True),
                                      wait=w,
                                      inc=("tE" if lastk and (o, n, stp) == segs[-1] else None))
                for (o, n, stp) in segs:
                    w = [("vE", msk_tok[kt])]
                    if first and (o, n, stp) == segs[0]:
                        w += [zr_free, ("vE", t_cst)]
                    z_last = bk.emit(T, "matmul",
                                     dict(out=ps[3][:, o:o + n],
                                          lhsT=ones,
                                          rhs=mex[kt % 4][:, o:o + n],
                                          start=first, stop=stp,
                                          skip_group_check=True),
                                     wait=w,
                                     inc=("tE" if lastk and (o, n, stp) == segs[-1] else None))
                mex_free[kt % 4] = ("tE", z_last) if z_last else None
        tln = bk.emit(A, "activation", dict(out=zrS, in_=ps[3], func=AF.Ln),
                      wait=[("tE", z_last)], inc="aE")
        tze = bk.emit(A, "activation", dict(out=zrS, in_=zrS, func=AF.Exp,
                                            scale=cst[:, 5:6]),
                      wait=[("aE", tln)], inc="aE")
        zr_free = ("aE", tln)
        tav = bk.emit(V, "tensor_tensor",
                      dict(out=avn[:, NQ * h:NQ * (h + 1)], in0=ps[2], in1=zrS,
                           op=ALU.mult),
                      wait=[("tE", av_last), ("aE", tze)], inc="vE")
        av_free = ("vE", tav)
        zr_free = ("vE", tav)
        kh_free[par] = ("tE", sc_tok[NST - 1])
        vh_free[par] = ("tE", av_last)
        if h + 2 < H:
            prefetch_head(h + 2)
        if h == H - 1:
            last_scores = sc_tok[NST - 1]

    for fc in (0, 1):
        wo_tok[fc] = load_col(wocol_r[fc], woutTd, fc, wait=[("tE", last_scores)])

    bk.gate()

    xrs_free = [None] * 4
    x1bf_free = None
    x1Tst_free = None
    pfree5 = [None] * 8
    ln_free = dict(st6=None, rr=None)
    for ttq in range(2):
        tts = [4 * ttq + i for i in range(4)]
        xr_tok = [None] * 4
        for i, tt in enumerate(tts):
            xr_tok[i] = bk.dma(A, f"dAx{i % 2}", xrs[i],
                               xresd[128 * tt:128 * (tt + 1), :],
                               wait=[xrs_free[i]])
        for fc in (2, 3, 0, 1):
            for i, tt in enumerate(tts):
                sl = (fc % 2) * 4 + i
                w0 = wo_tok[fc] + [wbq_tok[fc], pfree5[sl]]
                bk.emit(T, "matmul",
                        dict(out=pslot[sl], lhsT=ones[0:1, :],
                             rhs=bq[fc][0:1, :],
                             start=True, stop=False), wait=w0)
                mm = None
                for et in range(ET):
                    mm = bk.emit(T, "matmul",
                                 dict(out=pslot[sl],
                                      lhsT=avn[:, NQ * et + 128 * tt:NQ * et + 128 * tt + 128],
                                      rhs=wocol[fc][:, 512 * et:512 * (et + 1)],
                                      start=False, stop=(et == ET - 1)),
                                 inc=("tE" if et == ET - 1 else None))
                tv = bk.emit(V, "tensor_tensor",
                             dict(out=xrs[i][:, 512 * fc:512 * (fc + 1)],
                                  in0=pslot[sl],
                                  in1=xrs[i][:, 512 * fc:512 * (fc + 1)],
                                  op=ALU.add),
                             wait=[("tE", mm), xr_tok[i]],
                             inc="vE")
                pfree5[sl] = ("vE", tv)
                if fc == 1:
                    tb = None
                    for ck in range(4):
                        tb = bk.emit(V, "bn_stats",
                                     dict(out=st6[:, 6 * ck:6 * ck + 6],
                                          in_=xrs[i][:, 512 * ck:512 * (ck + 1)]),
                                     wait=([("vE", tv), ln_free["st6"]] if ck == 0
                                           else [("vE", tb)]),
                                     inc="vE")
                    ta = bk.emit(V, "bn_aggr", dict(out=ag2, in_=st6),
                                 wait=[("vE", tb)], inc="vE")
                    tl1 = bk.emit(A, "activation",
                                  dict(out=rr[:, 1:2], in_=ag2[:, 1:2], func=AF.Ln,
                                       bias=cst[:, 3:4]),
                                  wait=[("vE", ta), ln_free["rr"]], inc="aE")
                    tl2 = bk.emit(A, "activation",
                                  dict(out=rr[:, 2:3], in_=rr[:, 1:2], func=AF.Exp,
                                       scale=cst[:, 1:2]),
                                  wait=[("aE", tl1)], inc="aE")
                    ln_free["rr"] = ("aE", tl2)
                    ln_free["st6"] = ("vE", ta)
                    tf = bk.emit(V, "tensor_scalar",
                                 dict(out=xrs[i], in0=xrs[i], scalar1=ag2[:, 0:1],
                                      scalar2=rr[:, 2:3], op0=ALU.subtract,
                                      op1=ALU.mult),
                                 wait=[("aE", tl2)], inc="vE")
                    tfb = bk.emit(V, "tensor_copy",
                                  dict(out=x1bf, in_=xrs[i]),
                                  wait=[("vE", tf), x1bf_free], inc="vE")
                    tsp1 = bk.dma(A, "dAs1", x1sd[128 * tt:128 * (tt + 1), :], xrs[i],
                                  wait=[("vE", tf)])
                    xrs_free[i] = tsp1
                    ttr = bk.dma(SY, "dTx", x1Tst.rearrange("p (t c) -> p t c", t=ET),
                                 x1bf, transpose=True,
                                 wait=[("vE", tfb), x1Tst_free])
                    x1bf_free = ttr
                    tsp2 = bk.dma(A, "dAs2",
                                  x1Tsd[:, :, 128 * tt:128 * (tt + 1)]
                                  .rearrange("t p c -> p t c"),
                                  x1Tst.rearrange("p (t c) -> p t c", t=ET),
                                  wait=[ttr])
                    x1Tst_free = tsp2
            emit_gu_unit()
            emit_gu_unit()
    t_x1T_sp = x1Tst_free

    bk.gate()

    dq["ph6"] = True
    t_x1T = bk.dma(A, "dAxT", x1T.rearrange("p (t c) -> p t c", t=ET),
                   x1Tsd.rearrange("t p c -> p t c"), wait=[t_x1T_sp])
    x1T_r = x1T.rearrange("p (t q) -> p t q", t=ET)
    sil_free = [None, None]
    hstg_free = [None, None]
    pg_free = [None, None]
    pu_free = [None, None]
    hsp_tok = [None] * GU_PAIRS
    for pr in range(GU_PAIRS):
        par = pr % 2
        gf, uf = pr, GU_PAIRS + pr
        pg, pu = ps[2 * par], ps[2 * par + 1]
        mmg = None
        for ht in range(ET):
            for hseg in range(2):
                w = []
                if ht == 0 and hseg == 0:
                    w = [dq["gu_ready"][gf], t_x1T, pg_free[par]]
                mmg = bk.emit(T, "matmul",
                              dict(out=pg[:, 512 * hseg:512 * (hseg + 1)],
                                   lhsT=ring[dq["gu_slot"][gf]][:, 128 * ht:128 * (ht + 1)],
                                   rhs=x1T_r[:, ht, 512 * hseg:512 * (hseg + 1)],
                                   start=(ht == 0), stop=(ht == ET - 1)),
                              wait=w,
                              inc=("tE" if ht == ET - 1 and hseg == 1 else None))
        dq["ring_free"][dq["gu_slot"][gf]] = mmg
        mmu = None
        for ht in range(ET):
            for hseg in range(2):
                w = []
                if ht == 0 and hseg == 0:
                    w = [dq["gu_ready"][uf], pu_free[par]]
                mmu = bk.emit(T, "matmul",
                              dict(out=pu[:, 512 * hseg:512 * (hseg + 1)],
                                   lhsT=ring[dq["gu_slot"][uf]][:, 128 * ht:128 * (ht + 1)],
                                   rhs=x1T_r[:, ht, 512 * hseg:512 * (hseg + 1)],
                                   start=(ht == 0), stop=(ht == ET - 1)),
                              wait=w,
                              inc=("tE" if ht == ET - 1 and hseg == 1 else None))
        dq["ring_free"][dq["gu_slot"][uf]] = mmu
        tsl = bk.emit(A, "activation", dict(out=silwk[par], in_=pg, func=AF.Silu),
                      wait=[("tE", mmg), sil_free[par]], inc="aE")
        pg_free[par] = ("aE", tsl)
        th = bk.emit(V, "tensor_tensor",
                     dict(out=hstg[par], in0=silwk[par], in1=pu, op=ALU.mult),
                     wait=[("aE", tsl), ("tE", mmu), hstg_free[par]], inc="vE")
        pu_free[par] = ("vE", th)
        sil_free[par] = ("vE", th)
        tsp = bk.dma(A, f"dAhs{par}", hTsd[pr], hstg[par], wait=[("vE", th)])
        hstg_free[par] = tsp
        hsp_tok[pr] = tsp
        for _ in range(2):
            if not emit_gu_unit():
                emit_dn_unit()
        emit_dn_unit()
        if pr >= GU_PAIRS - 3:
            emit_dn_unit()
            emit_dn_unit()

    bk.gate()

    chk_tok = [None] * 3
    chk_free = [None] * 3
    chk_n = [0]

    def load_chunk(qq):
        par = chk_n[0] % 3
        chk_n[0] += 1
        w = [chk_free[par], hsp_tok[11 * qq + 10], hsp_tok[11 * qq + 9]]
        cr = chk[par].rearrange("p (i q) -> p i q", i=11)
        hr = hTsd[11 * qq:11 * (qq + 1)].rearrange("i p q -> p i q")
        ta = bk.dma(A, "dAht0", cr[:, 0:6], hr[:, 0:6], wait=w)
        tb = bk.dma(A, "dAht1", cr[:, 6:11], hr[:, 6:11], wait=w)
        chk_tok[par] = (ta, tb)
        return par

    x1r_tok = [None] * 4
    fo_free = [None, None]
    x1rl_started = [False]
    chk1_last = [None]

    def maybe_prefetch_x1rl():
        if x1rl_started[0] or dq["dn_next"] < DN_HT * 4:
            return
        x1rl_started[0] = True
        w = list(dq["dn_pk_free"]) + list(dq["dn_wq_free"])
        x1r_tok[0] = bk.dma(A, "dAxr0", x1rl[0], x1sd[0:128, :], wait=w)
        x1r_tok[1] = bk.dma(A, "dAxr1", x1rl[1], x1sd[128:256, :], wait=w)

    def prefetch_x1rl23():
        x1r_tok[2] = bk.dma(A, "dAxr0", x1rl[2], x1sd[256:384, :],
                            wait=[chk1_last[0]])
        x1r_tok[3] = bk.dma(A, "dAxr1", x1rl[3], x1sd[384:512, :],
                            wait=[chk1_last[0]])

    def emit_out_tile(tt, tev, tstat):
        par = tt % 2
        bpar = tt % 4
        ytile = yres[:, 2048 * tt:2048 * (tt + 1)]
        ta = bk.emit(V, "bn_aggr", dict(out=ag2, in_=st6y[:, 24 * tt:24 * (tt + 1)]),
                     wait=[("vE", tstat), ln_free["st6"]], inc="vE")
        tu = bk.emit(V, "tensor_scalar",
                     dict(out=rr[:, 0:1], in0=ag2[:, 0:1], scalar1=ag2[:, 0:1],
                          scalar2=ag2[:, 1:2], op0=ALU.mult, op1=ALU.add),
                     wait=[("vE", ta), ln_free["rr"]], inc="vE")
        tl1 = bk.emit(A, "activation", dict(out=rr[:, 1:2], in_=rr[:, 0:1], func=AF.Ln,
                                            bias=cst[:, 4:5]),
                      wait=[("vE", tu)], inc="aE")
        tl2 = bk.emit(A, "activation", dict(out=rr[:, 2:3], in_=rr[:, 1:2], func=AF.Exp,
                                            scale=cst[:, 1:2]),
                      wait=[("aE", tl1)], inc="aE")
        tsum = bk.emit(V, "scalar_tensor_tensor",
                       dict(out=outst[par], in0=ytile, scalar=rr[:, 2:3], in1=x1rl[bpar],
                            op0=ALU.mult, op1=ALU.add),
                       wait=[("aE", tl2), x1r_tok[bpar], fo_free[par]], inc="vE")
        if tt + 4 < NTT:
            x1r_tok[bpar] = bk.dma(A, f"dAxr{par}", x1rl[bpar],
                                   x1sd[128 * (tt + 4):128 * (tt + 5), :],
                                   wait=[("vE", tsum)])
        tb2 = None
        for ck in range(4):
            tb2 = bk.emit(V, "bn_stats",
                          dict(out=st6[:, 6 * ck:6 * ck + 6],
                               in_=outst[par][:, 512 * ck:512 * (ck + 1)]),
                          wait=([("vE", tsum)] if ck == 0 else [("vE", tb2)]),
                          inc="vE")
        ta2 = bk.emit(V, "bn_aggr", dict(out=ag2, in_=st6), wait=[("vE", tb2)], inc="vE")
        tl3 = bk.emit(A, "activation", dict(out=rr[:, 1:2], in_=ag2[:, 1:2], func=AF.Ln,
                                            bias=cst[:, 3:4]),
                      wait=[("vE", ta2)], inc="aE")
        tl4 = bk.emit(A, "activation", dict(out=rr[:, 2:3], in_=rr[:, 1:2], func=AF.Exp,
                                            scale=cst[:, 1:2]),
                      wait=[("aE", tl3)], inc="aE")
        ln_free["rr"] = ("aE", tl4)
        ln_free["st6"] = ("vE", ta2)
        tbn = bk.emit(V, "tensor_scalar",
                      dict(out=rr[:, 3:4], in0=ag2[:, 0:1], scalar1=rr[:, 2:3],
                           scalar2=-1.0, op0=ALU.mult, op1=ALU.mult),
                      wait=[("aE", tl4)], inc="vE")
        tout = bk.emit(A, "activation",
                       dict(out=outst[par], in_=outst[par], func=AF.Identity,
                            scale=rr[:, 2:3], bias=rr[:, 3:4]),
                       wait=[("vE", tbn)], inc="aE")
        out_tok = bk.dma(P, f"dPo{par}", outd[128 * tt:128 * (tt + 1), :], outst[par],
                         wait=[("aE", tout)])
        fo_free[par] = out_tok

    y_free = [None] * 8
    load_chunk(0)
    load_chunk(1)
    for c in range(4):
        mml = [None] * 8
        for qq in range(4):
            gtok = dq["dn_ready"][(c, qq, 3)]
            wgr = wdq[(4 * c + qq) % 2].rearrange("p (iq c2) -> p iq c2", iq=11)
            par = (4 * c + qq) % 3
            cbuf = chk[par]
            ctok = chk_tok[par]
            last = (c == 3 and qq == 3)
            order = ([(iq, tt) for tt in range(8) for iq in range(11)] if last
                     else [(iq, tt) for iq in range(11) for tt in range(8)])
            for (iq, tt) in order:
                w = []
                if iq == 0 and (tt == 0 or last):
                    w = [*ctok, gtok]
                if qq == 0 and iq == 0:
                    w = w + (y_free[tt] or [])
                mml[tt] = bk.emit(T, "matmul",
                                  dict(out=pslot[tt],
                                       lhsT=cbuf[:, NQ * iq + 128 * tt:
                                                 NQ * iq + 128 * (tt + 1)],
                                       rhs=wgr[:, iq],
                                       start=(qq == 0 and iq == 0),
                                       stop=(qq == 3 and iq == 10)),
                                  wait=w,
                                  inc=("tE" if (iq == 10 or tt == 7) else None))
                if last and iq == 10:
                    tev = bk.emit(A, "copy",
                                  dict(out=yres[:, 2048 * tt + 512 * c:
                                                2048 * tt + 512 * (c + 1)],
                                       in_=pslot[tt]),
                                  wait=[("tE", mml[tt])], inc="aE")
                    tstat = bk.emit(V, "bn_stats",
                                    dict(out=st6y[:, 24 * tt + 6 * c:24 * tt + 6 * (c + 1)],
                                         in_=pslot[tt]),
                                    wait=[("tE", mml[tt]), ("aE", tev)], inc="vE")
                    emit_out_tile(tt, tev, tstat)
            chk_free[par] = ("tE", mml[7])
            if c == 3 and qq == 1:
                chk1_last[0] = ("tE", mml[7])
                prefetch_x1rl23()
            dq["dn_grp_free"][(4 * c + qq) % 2] = mml[7]
            for _ in range(4):
                emit_dn_unit()
            maybe_prefetch_x1rl()
            if 4 * c + qq <= 13:
                load_chunk((qq + 2) % 4)
        if c < 3:
            for tt in range(8):
                eng2 = A if tt % 2 == 0 else V
                tev = bk.emit(eng2, "copy" if tt % 2 == 0 else "tensor_copy",
                              dict(**({"out": yres[:, 2048 * tt + 512 * c:
                                                   2048 * tt + 512 * (c + 1)],
                                       } |
                                      ({"in_": pslot[tt]}))),
                              wait=[("tE", mml[tt])], inc="aE" if tt % 2 == 0 else "vE")
                tstat = bk.emit(V, "bn_stats",
                                dict(out=st6y[:, 24 * tt + 6 * c:24 * tt + 6 * (c + 1)],
                                     in_=pslot[tt]),
                                wait=[("tE", mml[tt])] +
                                     ([("aE", tev)] if tt % 2 == 0 else []),
                                inc="vE")
                y_free[tt] = [("aE" if tt % 2 == 0 else "vE", tev), ("vE", tstat)]
    bk.emit(P, "wait_only", {}, wait=[fo_free[0], fo_free[1]])

    return nc, bk


def finalize(nc, bk):
    with nc.Block() as blk:
        @blk.tensor
        def _(e):
            bk.replay(e, T)

        @blk.scalar
        def _(e):
            bk.replay(e, A)

        @blk.vector
        def _(e):
            bk.replay(e, V)

        @blk.gpsimd
        def _(e):
            bk.replay(e, P)

        @blk.sync
        def _(e):
            bk.replay(e, SY)
    return nc


def _perm_half(n, chunk=None):
    if chunk is None:
        chunk = n
    p = np.arange(n)
    qq = p // chunk
    w = p % chunk
    half = chunk // 2
    return chunk * qq + np.where(w < half, 2 * w, 2 * (w - half) + 1)


def _host_prep(inputs):
    x = np.asarray(inputs["L0_attn_x"], dtype=np.float32)
    Wqkv = np.asarray(inputs["L0_attn_Wqkv"], dtype=np.float32)
    bqkv = np.asarray(inputs["L0_attn_bqkv"], dtype=np.float32)
    Wout = np.asarray(inputs["L0_attn_Wout"], dtype=np.float32)
    bout = np.asarray(inputs["L0_attn_bout"], dtype=np.float32)
    gup = np.ascontiguousarray(np.asarray(inputs["L0_ffn_gate_up_packed"], dtype=np.int32))
    gusc = np.asarray(inputs["L0_ffn_gate_up_scales"])
    dnp = np.ascontiguousarray(np.asarray(inputs["L0_ffn_down_packed"], dtype=np.int32))
    dnsc = np.asarray(inputs["L0_ffn_down_scales"])

    Pgu = _perm_half(E)
    Pdn = _perm_half(IFF, 1408)

    inv = (10000.0 ** (-(np.arange(0, E, 2, dtype=np.float32) / E))).astype(np.float32)
    emb = np.repeat(inv, 2)
    ang = np.float32(S) * emb
    sin = np.sin(ang, dtype=np.float32)
    cos = np.cos(ang, dtype=np.float32)
    sinS = np.concatenate([-sin[0:E // 2], sin[E // 2:]]).astype(np.float32)
    cosR = np.ascontiguousarray(np.broadcast_to(cos, (128, E)))
    sinSR = np.ascontiguousarray(np.broadcast_to(sinS, (128, E)))

    WqkvT = np.ascontiguousarray(Wqkv[:, Pgu].T)
    WoutT = np.ascontiguousarray(Wout[Pgu, :].T)
    bqkvR = np.ascontiguousarray(bqkv[None, :])
    boutR = np.ascontiguousarray(bout[Pgu][None, :])

    gup_dev = np.ascontiguousarray(
        np.concatenate([gup[0:IFF][Pdn], gup[IFF:][Pdn]], axis=0))
    gus_sm = np.ascontiguousarray(
        np.concatenate([gusc.T[0:IFF][Pdn], gusc.T[IFF:][Pdn]], axis=0)).astype(np.float16)
    dnp_dev = np.ascontiguousarray(dnp[Pgu])
    dns_sm = np.ascontiguousarray(dnsc.T[Pgu]).astype(np.float16)
    cg = np.arange(2048)
    gus_dev = np.ascontiguousarray(gus_sm[:, (cg % 1024) // 64])
    cd = np.arange(IFF)
    dns_dev = np.ascontiguousarray(dns_sm[:, (cd // 1408) * 11 + (cd % 704) // 64])

    in_maps = []
    for c in range(8):
        b, r = c // 2, c % 2
        xd = x[b][:, Pgu]
        xT = np.ascontiguousarray(xd.T)
        xTq = np.ascontiguousarray(xT[:, r::2])
        xres = np.ascontiguousarray(xd[r::2, :])
        kl = np.arange(128)[:, None]
        ql = np.arange(128)[None, :]
        maskAB = np.ascontiguousarray(np.concatenate([
            (kl <= 2 * ql + r), (kl + 128 <= 2 * ql + r)],
            axis=1).astype(ml_dtypes.bfloat16))
        in_maps.append(dict(
            xT=xT, xTq=xTq, xres=xres, wqkvT=WqkvT, bqkvR=bqkvR, woutT=WoutT,
            boutR=boutR, cosR=cosR, sinSR=sinSR, maskAB=maskAB,
            gup=gup_dev, gus=gus_dev, dnp=dnp_dev, dns=dns_dev))
    return in_maps, Pgu


_CACHE = {}


def kernel(**inputs):
    in_maps, Pgu = _host_prep(inputs)
    if "nc" not in _CACHE:
        nc, bk = build_program()
        finalize(nc, bk)
        _CACHE["nc"] = nc
    nc = _CACHE["nc"]
    res = run_bass_kernel_spmd(nc, in_maps, core_ids=list(range(8)))
    out = np.empty((4, S, E), dtype=np.float32)
    for c in range(8):
        b, r = c // 2, c % 2
        oc = res.results[c]["out"]
        out[b, r::2][:, Pgu] = oc
    return out
